# version 02
# baseline (speedup 1.0000x reference)
import numpy as np
import ml_dtypes
import concourse.bass as bass
import concourse.mybir as mybir
from contextlib import ExitStack

dt = mybir.dt
AF = mybir.ActivationFunctionType
ALU = mybir.AluOpType
AX = mybir.AxisListType.X
F32, BF16, I32, U8 = dt.float32, dt.bfloat16, dt.int32, dt.uint8
BF = ml_dtypes.bfloat16

P = 128
T = 8192
C = 2048
NH = 16
D = 128
NS = 8
TOK = NS * P
MLP = 16384
NCC = MLP // P
HC = C // P
SCALE = float(1.0 / np.sqrt(128.0))
EXP_SHIFT = -12.0
EPS = 1e-5

PI = np.concatenate([np.arange(0, C, 2), np.arange(1, C, 2)])
PIC = np.concatenate([np.arange(0, MLP, 2), np.arange(1, MLP, 2)])


def own_rows(core):
    return np.concatenate(
        [np.arange(P * (8 * s + core), P * (8 * s + core) + P) for s in range(NS)])


def host_prep(inputs, core):
    x = np.asarray(inputs["L0_attn_x"])[0]
    w_attn = np.asarray(inputs["L0_attn_w_attn"])
    w_proj = np.asarray(inputs["L0_attn_w_proj"])
    gW = np.asarray(inputs["L0_ffn_gate_W_packed"])
    gS = np.asarray(inputs["L0_ffn_gate_scales"])
    uW = np.asarray(inputs["L0_ffn_up_W_packed"])
    uS = np.asarray(inputs["L0_ffn_up_scales"])
    dW = np.asarray(inputs["L0_ffn_down_W_packed"])
    dS = np.asarray(inputs["L0_ffn_down_scales"])
    rows = own_rows(core)
    m = {}
    m["x_full"] = np.ascontiguousarray(x, dtype=np.float32)
    m["x_own"] = np.ascontiguousarray(x[rows], dtype=np.float32)
    m["w_attn_p"] = np.ascontiguousarray(w_attn[PI, :], dtype=np.float32)
    m["w_proj_p"] = np.ascontiguousarray(w_proj[:, PI], dtype=np.float32)
    m["gWT"] = np.ascontiguousarray(gW[PIC].astype(np.uint8).T)
    m["uWT"] = np.ascontiguousarray(uW[PIC].astype(np.uint8).T)
    m["dWT"] = np.ascontiguousarray(dW[PI].astype(np.uint8).T)
    m["gST"] = np.ascontiguousarray(gS[PIC].T, dtype=np.float32)
    m["uST"] = np.ascontiguousarray(uS[PIC].T, dtype=np.float32)
    m["dST"] = np.ascontiguousarray(dS[PI].T, dtype=np.float32)
    m["ident32"] = np.eye(P, dtype=np.float32)
    m["identbf"] = np.eye(P, dtype=BF)
    kk = np.arange(P)[None, :, None]
    qq = np.arange(P)[None, None, :]
    oo = np.arange(NS)[:, None, None]
    m["masks"] = (128 * oo + kk <= 128 * core + qq).astype(BF)
    R = np.zeros((8, P), np.float32)
    for g_ in range(8):
        R[g_, 16 * g_:16 * g_ + 16] = float(2.0 ** 126)
    m["repl"] = R
    return m


class Sem:
    __slots__ = ("h", "n")

    def __init__(self, h):
        self.h = h
        self.n = 0

    def inc(self, instr, amt=1):
        instr.then_inc(self.h, amt)
        self.n += amt
        return self.n


class Builder:
    def __init__(self, phase=99):
        self.phase = phase
        self.nc = bass.Bass("TRN2", target_bir_lowering=False)
        self.es = ExitStack()
        self.es.__enter__()
        self._semid = 0

    def sb(self, name, shape, dtype, scope=None):
        return (scope or self.es).enter_context(self.nc.sbuf_tensor(name, shape, dtype))

    def psum(self, name, shape, dtype=F32, scope=None):
        return (scope or self.es).enter_context(self.nc.psum_tensor(name, shape, dtype))

    def sem(self, name=None):
        self._semid += 1
        h = self.es.enter_context(self.nc.semaphore(name or f"s{self._semid}"))
        return Sem(h)

    def dram(self, name, shape, dtype, kind="Internal"):
        return self.nc.dram_tensor(name, shape, dtype, kind=kind).ap()

    def build(self):
        nc = self.nc
        self.x_full = self.dram("x_full", [T, C], F32, "ExternalInput")
        self.x_own = self.dram("x_own", [TOK, C], F32, "ExternalInput")
        self.w_attn_p = self.dram("w_attn_p", [C, 3 * C], F32, "ExternalInput")
        self.w_proj_p = self.dram("w_proj_p", [C, C], F32, "ExternalInput")
        self.gWT = self.dram("gWT", [C // 2, MLP], U8, "ExternalInput")
        self.uWT = self.dram("uWT", [C // 2, MLP], U8, "ExternalInput")
        self.dWT = self.dram("dWT", [MLP // 2, C], U8, "ExternalInput")
        self.gST = self.dram("gST", [64, MLP], F32, "ExternalInput")
        self.uST = self.dram("uST", [64, MLP], F32, "ExternalInput")
        self.dST = self.dram("dST", [512, C], F32, "ExternalInput")
        self.ident32 = self.dram("ident32", [P, P], F32, "ExternalInput")
        self.identbf = self.dram("identbf", [P, P], BF16, "ExternalInput")
        self.masks = self.dram("masks", [NS, P, P], BF16, "ExternalInput")
        self.repl = self.dram("repl", [8, P], F32, "ExternalInput")

        self.d_ln1tm = self.dram("d_ln1tm", [T, C], BF16)
        self.d_ln1fmT = self.dram("d_ln1fmT", [C, T], BF16)
        self.d_lo = self.dram("d_lo", [C, TOK], BF16)
        self.d_xfm = self.dram("d_xfm", [C, TOK], F32)
        self.d_K = self.dram("d_K", [C, T], BF16)
        self.d_V = self.dram("d_V", [T, C], BF16)
        self.d_Q = self.dram("d_Q", [C, TOK], BF16)
        self.d_O = self.dram("d_O", [C, TOK], BF16)
        self.d_x1 = self.dram("d_x1", [C, TOK], F32)
        self.d_ln2 = self.dram("d_ln2", [C, TOK], BF16)
        self.d_Wg = self.dram("d_Wg", [NCC, HC, P, P], BF16)
        self.d_Wu = self.dram("d_Wu", [NCC, HC, P, P], BF16)
        self.d_Wd = self.dram("d_Wd", [NCC, HC, P, P], BF16)
        self.d_dbg = self.dram("d_dbg", [4 * P, C], F32)
        self.d_dbE = self.dram("d_dbE", [P, C], BF16)
        self.d_dbi = self.dram("d_dbi", [P, NCC * 512], BF16)
        self.d_dbx = self.dram("d_dbx", [4 * P, TOK], F32)
        self.d_dbo = self.dram("d_dbo", [2 * P, HC * 512], F32)
        self.d_dbL = self.dram("d_dbL", [P, C], BF16)

        self.y = self.dram("y", [TOK, C], F32, "ExternalOutput")

        self.identb_s = self.sb("identb_s", [P, P], BF16)
        self.ident3_s = self.sb("ident3_s", [P, P], F32)
        self.ones_bf = self.sb("ones_bf", [P, 1], BF16)
        self.masks_s = self.sb("masks_s", [P, NS * P], BF16)
        self.repl_s = self.sb("repl_s", [8, P], F32)
        self.consts = self.sb("consts", [P, 2], F32)

        s, v = nc.sync, nc.vector
        ld0 = self.sem("ld0")
        ld0.inc(s.dma_start(out=self.identb_s[:], in_=self.identbf[:]), 16)
        ld0.inc(s.dma_start(out=self.ident3_s[:], in_=self.ident32[:]), 16)
        ld0.inc(s.dma_start(out=self.masks_s[:].rearrange("k (o q) -> k o q", q=P),
                            in_=self.masks[:].rearrange("o k q -> k o q")), 16)
        ld0.inc(s.dma_start(out=self.repl_s[:], in_=self.repl[:]), 16)
        self.ones_r32 = self.sb("ones_r32", [1, P], F32)
        self.ones_c32 = self.sb("ones_c32", [P, 1], F32)
        v.wait_ge(ld0.h, ld0.n)
        v.memset(self.ones_bf[:], 1.0)
        v.memset(self.ones_r32[:], 1.0)
        v.memset(self.ones_c32[:], 1.0)
        self.c_rdy = self.sem("c_rdy")
        v.memset(self.consts[:, 0:1], EXP_SHIFT)
        nc.const_aps.register(F32, EXP_SHIFT, self.consts[:, 0:1])
        self.c_rdy.inc(v.memset(self.consts[:, 1:2], SCALE), 1)
        nc.const_aps.register(F32, SCALE, self.consts[:, 1:2])
        self.ld0 = ld0

        with ExitStack() as sc1:
            self.ph1_ln1(sc1)
        if self.phase >= 2:
            with ExitStack() as sc2:
                self.ph2_bounce(sc2)
        if self.phase >= 3:
            with ExitStack() as sc3:
                self.ph3_kvq(sc3)
        if self.phase >= 4:
            with ExitStack() as sc4:
                self.ph4_dequant(sc4)
        if self.phase >= 5:
            with ExitStack() as sc5:
                self.ph5_attention(sc5)
        if self.phase >= 6:
            with ExitStack() as sc6:
                self.ph6_proj_ln2(sc6)
        if self.phase >= 7:
            with ExitStack() as sc7:
                self.ph7_ffn(sc7)
        with ExitStack() as sce:
            self.epilogue(sce)
        self.es.__exit__(None, None, None)
        return nc

    def _ln_tile(self, xin, lnout, st, sq, s_act, s_cm):
        nc = self.nc
        v, sc, pool = nc.vector, nc.scalar, nc.gpsimd
        wv = s_cm.inc(v.reduce_sum(st[:, 0:1], xin, axis=AX), 1)
        wa = s_act.inc(sc.activation(sq[:], xin, AF.Square, accum_out=st[:, 1:2]), 1)
        pool.wait_ge(s_cm.h, wv)
        pool.wait_ge(s_act.h, wa)
        pool.tensor_scalar(st[:, 2:3], st[:, 0:1], 1.0 / C, None, ALU.mult)
        pool.tensor_scalar(st[:, 3:4], st[:, 1:2], 1.0 / C, None, ALU.mult)
        pool.tensor_tensor(st[:, 4:5], st[:, 2:3], st[:, 2:3], ALU.mult)
        pool.tensor_tensor(st[:, 3:4], st[:, 3:4], st[:, 4:5], ALU.subtract)
        wp = s_cm.inc(pool.tensor_scalar(st[:, 3:4], st[:, 3:4], EPS, None, ALU.add), 1)
        sc.wait_ge(s_cm.h, wp)
        wa = s_act.inc(sc.activation(st[:, 5:6], st[:, 3:4], AF.Ln), 1)
        sc.wait_ge(s_act.h, wa)
        wa = s_act.inc(sc.activation(st[:, 5:6], st[:, 5:6], AF.Exp, scale=-0.5), 1)
        pool.wait_ge(s_act.h, wa)
        pool.tensor_tensor(st[:, 6:7], st[:, 2:3], st[:, 5:6], ALU.mult)
        pool.tensor_scalar(st[:, 6:7], st[:, 6:7], -1.0, None, ALU.mult)
        pool.tensor_scalar(lnout[:, 0:C // 2], xin[:, 0:C:2],
                           st[:, 5:6], st[:, 6:7], ALU.mult, ALU.add)
        return pool.tensor_scalar(lnout[:, C // 2:C], xin[:, 1:C:2],
                                  st[:, 5:6], st[:, 6:7], ALU.mult, ALU.add)

    def ph1_ln1(self, scope):
        nc = self.nc
        s, v, sc, g, t = nc.sync, nc.vector, nc.scalar, nc.gpsimd, nc.tensor
        NB = 3
        xt = [self.sb(f"p1x{i}", [P, C], F32, scope) for i in range(NB)]
        lt = [self.sb(f"p1l{i}", [P, C], BF16, scope) for i in range(NB)]
        sq = self.sb("p1sq", [P, C], F32, scope)
        stats = [self.sb(f"p1st{i}", [P, 8], F32, scope) for i in range(NB)]
        ld = self.sem("p1ld")
        act = self.sem("p1act")
        cm = self.sem("p1cm")
        ap = self.sem("p1ap")
        so = self.sem("p1so")
        ntile = T // P
        for i in range(ntile):
            b = i % NB
            if i >= NB:
                s.wait_ge(so.h, 16 * (i - NB + 1))
            wl = ld.inc(s.dma_start(out=xt[b][:], in_=self.x_full[P * i:P * (i + 1), :]), 16)
            v.wait_ge(ld.h, wl)
            sc.wait_ge(ld.h, wl)
            g.wait_ge(ld.h, wl)
            if i >= NB:
                v.wait_ge(ap.h, 2 * (i - NB + 1))
                g.wait_ge(so.h, 16 * (i - NB + 1))
            wp = ap.inc(self._ln_tile(xt[b][:], lt[b][:], stats[b], sq, act, cm), 2)
            s.wait_ge(ap.h, wp)
            so.inc(s.dma_start(out=self.d_ln1tm[P * i:P * (i + 1), :], in_=lt[b][:]), 16)
        self.ph1_full_done = so

        xo = [self.sb(f"p1ox{i}", [P, C], F32, scope) for i in range(2)]
        lo = [self.sb(f"p1ol{i}", [P, C], BF16, scope) for i in range(2)]
        sqo = self.sb("p1osq", [P, C], F32, scope)
        stato = [self.sb(f"p1ost{i}", [P, 8], F32, scope) for i in range(2)]
        evx = self.sb("p1oevx", [P, P], F32, scope)
        evl = self.sb("p1oevl", [P, P], BF16, scope)
        ldo = self.sem("p1old")
        acto = self.sem("p1oact")
        cmo = self.sem("p1ocm")
        apo = self.sem("p1oap")
        mmo = self.sem("p1omm")
        evo = self.sem("p1oev")
        soo = self.sem("p1oso")
        ps32 = self.psum("p1ops32", [P, P], F32, scope)
        psbf = self.psum("p1opsbf", [P, P], BF16, scope)
        for i in range(NS):
            b = i % 2
            if i >= 2:
                g.wait_ge(soo.h, 32 * HC * (i - 1))
            wl = ldo.inc(g.dma_start(out=xo[b][:], in_=self.x_own[P * i:P * (i + 1), :]), 16)
            v.wait_ge(ldo.h, wl)
            sc.wait_ge(ldo.h, wl)
            g.wait_ge(ldo.h, wl)
            if i >= 2:
                v.wait_ge(apo.h, 2 * (i - 1))
            wp = apo.inc(self._ln_tile(xo[b][:], lo[b][:], stato[b], sqo, acto, cmo), 2)
            t.wait_ge(ldo.h, wl)
            t.wait_ge(apo.h, wp)
            for hc in range(HC):
                if hc < 8:
                    cols = xo[b][:, 2 * P * hc: 2 * P * (hc + 1): 2]
                else:
                    cols = xo[b][:, 1 + 2 * P * (hc - 8): 2 * P * (hc - 7): 2]
                if i > 0 or hc > 0:
                    t.wait_ge(evo.h, evo.n)
                mmo.inc(t.transpose(ps32[:], cols, self.ident3_s[:]), 1)
                wm = mmo.inc(t.transpose(psbf[:], lo[b][:, P * hc:P * (hc + 1)],
                                         self.identb_s[:]), 1)
                v.wait_ge(mmo.h, wm)
                if i > 0 or hc > 0:
                    v.wait_ge(soo.h, soo.n)
                evo.inc(v.tensor_copy(evx[:], ps32[:]), 1)
                we = evo.inc(v.tensor_copy(evl[:], psbf[:]), 1)
                s.wait_ge(evo.h, we)
                soo.inc(s.dma_start(out=self.d_xfm[P * hc:P * (hc + 1),
                                                   P * i:P * (i + 1)], in_=evx[:]), 16)
                soo.inc(s.dma_start(out=self.d_lo[P * hc:P * (hc + 1),
                                                  P * i:P * (i + 1)], in_=evl[:]), 16)
        self.ph1_own_done = soo

    def ph2_bounce(self, scope):
        nc = self.nc
        s = nc.sync
        tb = [self.sb(f"p2t{i}", [P, T], BF16, scope) for i in range(2)]
        ld = self.sem("p2ld")
        so = self.sem("p2so")
        s.wait_ge(self.ph1_full_done.h, self.ph1_full_done.n)
        s.wait_ge(self.ph1_own_done.h, self.ph1_own_done.n)
        for k in range(HC):
            b = k % 2
            if k >= 2:
                s.wait_ge(so.h, 16 * (k - 1))
            for jt in range(16):
                wl = ld.inc(s.dma_start_transpose(
                    out=tb[b][:, 512 * jt:512 * (jt + 1)],
                    in_=self.d_ln1tm[512 * jt:512 * (jt + 1), P * k:P * (k + 1)]), 16)
            s.wait_ge(ld.h, wl)
            so.inc(s.dma_start(out=self.d_ln1fmT[P * k:P * (k + 1), :], in_=tb[b][:]), 16)
        self.ph2_done = so

    def _proj_pass(self, scope, tag, w_cols, out_mode, after=()):
        nc = self.nc
        s, g, v, sc, t = nc.sync, nc.gpsimd, nc.vector, nc.scalar, nc.tensor
        for eng in (s, g, v, sc, t):
            for sem_ in after:
                eng.wait_ge(sem_.h, sem_.n)
        W = 2048
        wsb = self.sb(f"{tag}w", [P, HC * W], BF16, scope)
        ldw = self.sem(f"{tag}ldw")
        for hc in range(HC):
            ldw.inc(g.dma_start(out=wsb[:, hc * W:(hc + 1) * W],
                                in_=self.w_attn_p[P * hc:P * (hc + 1),
                                                  w_cols[0]:w_cols[1]]), 16)
        rb = [self.sb(f"{tag}r{i}", [P, HC * 512], BF16, scope) for i in range(2)]
        ev = [self.sb(f"{tag}e{i}", [P, 512], BF16, scope) for i in range(4)]
        pss = [self.psum(f"{tag}p{i}", [P, 512], F32, scope) for i in range(4)]
        ld = self.sem(f"{tag}ld")
        mme = self.sem(f"{tag}mm")
        evs = self.sem(f"{tag}ev")
        sos = self.sem(f"{tag}so")
        t.wait_ge(ldw.h, ldw.n)
        grp = 0
        tr_mme = []
        NTR = T // 512
        for tr in range(NTR):
            b = tr % 2
            if tr >= 2:
                s.wait_ge(mme.h, tr_mme[tr - 2])
            for hcl in range(HC):
                wl = ld.inc(s.dma_start(
                    out=rb[b][:, 512 * hcl:512 * (hcl + 1)],
                    in_=self.d_ln1fmT[P * hcl:P * (hcl + 1),
                                      512 * tr:512 * (tr + 1)]), 16)
            t.wait_ge(ld.h, wl)
            if out_mode == "fm":
                sub = [("fm", kc) for kc in range(HC)]
            else:
                sub = [("tm", (tt, vc)) for tt in range(4) for vc in range(4)]
            for mode, idx in sub:
                pb = grp % 4
                if grp >= 4:
                    t.wait_ge(evs.h, grp - 3)
                if mode == "fm":
                    kc = idx
                    for hc in range(HC):
                        mm = t.matmul(pss[pb][:],
                                      wsb[:, hc * W + P * kc: hc * W + P * (kc + 1)],
                                      rb[b][:, hc * 512:(hc + 1) * 512],
                                      start=(hc == 0), stop=(hc == HC - 1))
                else:
                    tt, vc = idx
                    for hc in range(HC):
                        mm = t.matmul(pss[pb][:],
                                      rb[b][:, hc * 512 + P * tt: hc * 512 + P * (tt + 1)],
                                      wsb[:, hc * W + 512 * vc: hc * W + 512 * (vc + 1)],
                                      start=(hc == 0), stop=(hc == HC - 1))
                wm = mme.inc(mm, 1)
                eng = v if grp % 2 == 0 else sc
                eng.wait_ge(mme.h, wm)
                if grp >= 4:
                    eng.wait_ge(sos.h, 16 * (grp - 3))
                if eng is v:
                    we = evs.inc(v.tensor_copy(ev[pb][:], pss[pb][:]), 1)
                else:
                    we = evs.inc(sc.copy(ev[pb][:], pss[pb][:]), 1)
                s.wait_ge(evs.h, we)
                if mode == "fm":
                    dst = self.d_K[P * idx:P * (idx + 1), 512 * tr:512 * (tr + 1)]
                else:
                    tt, vc = idx
                    dst = self.d_V[512 * tr + P * tt: 512 * tr + P * (tt + 1),
                                   512 * vc:512 * (vc + 1)]
                sos.inc(s.dma_start(out=dst, in_=ev[pb][:]), 16)
                grp += 1
            tr_mme.append(mme.n)
        return sos

    def ph3_kvq(self, scope):
        nc = self.nc
        s, g, v, t = nc.sync, nc.gpsimd, nc.vector, nc.tensor
        with ExitStack() as kp:
            self.ph3_k_done = self._proj_pass(
                kp, "p3k", (C, 2 * C), "fm",
                after=(self.ph1_full_done, self.ph1_own_done, self.ph2_done))
        with ExitStack() as vp:
            self.ph3_done = self._proj_pass(vp, "p3v", (2 * C, 3 * C), "tm",
                                            after=(self.ph3_k_done,))
        with ExitStack() as qp:
            for eng in (s, g, v, t):
                eng.wait_ge(self.ph3_done.h, self.ph3_done.n)
            wq = self.sb("p3qw", [P, HC * C], BF16, qp)
            ldw = self.sem("p3qldw")
            for hc in range(HC):
                ldw.inc(g.dma_start(out=wq[:, hc * C:(hc + 1) * C],
                                    in_=self.w_attn_p[P * hc:P * (hc + 1), 0:C]), 16)
            lob = self.sb("p3qlo", [P, HC * 512], BF16, qp)
            ev = [self.sb(f"p3qe{i}", [P, 512], BF16, qp) for i in range(2)]
            pss = [self.psum(f"p3qp{i}", [P, 512], F32, qp) for i in range(2)]
            ld = self.sem("p3qld")
            mme = self.sem("p3qmm")
            evs = self.sem("p3qev")
            sos = self.sem("p3qso")
            t.wait_ge(ldw.h, ldw.n)
            g.wait_ge(self.ph1_own_done.h, self.ph1_own_done.n)
            grp = 0
            for half in range(2):
                if half == 1:
                    g.wait_ge(mme.h, mme.n)
                for hcl in range(HC):
                    wl = ld.inc(g.dma_start(
                        out=lob[:, 512 * hcl:512 * (hcl + 1)],
                        in_=self.d_lo[P * hcl:P * (hcl + 1),
                                      512 * half:512 * (half + 1)]), 16)
                t.wait_ge(ld.h, wl)
                for h in range(NH):
                    pb = grp % 2
                    if grp >= 2:
                        t.wait_ge(evs.h, grp - 1)
                    for hc in range(HC):
                        mm = t.matmul(pss[pb][:],
                                      wq[:, hc * C + P * h: hc * C + P * (h + 1)],
                                      lob[:, hc * 512:(hc + 1) * 512],
                                      start=(hc == 0), stop=(hc == HC - 1))
                    wm = mme.inc(mm, 1)
                    v.wait_ge(mme.h, wm)
                    if grp >= 2:
                        v.wait_ge(sos.h, 16 * (grp - 1))
                    we = evs.inc(v.tensor_copy(ev[pb][:], pss[pb][:]), 1)
                    s.wait_ge(evs.h, we)
                    sos.inc(s.dma_start(
                        out=self.d_Q[P * h:P * (h + 1), 512 * half:512 * (half + 1)],
                        in_=ev[pb][:]), 16)
                    grp += 1
            self.ph3_q_done = sos

    def _dequant_mat(self, scope, tag, src, scalesT, JC, CW, dest, hc_of):
        nc = self.nc
        s, g, v, t = nc.sync, nc.gpsimd, nc.vector, nc.tensor
        F = 2048
        bi = [self.sb(f"{tag}b{i}", [P, F], I32, scope) for i in range(2)]
        tp = self.sb(f"{tag}tp", [P, F], I32, scope)
        tq = self.sb(f"{tag}tq", [P, F], I32, scope)
        ty = self.sb(f"{tag}ty", [P, F], I32, scope)
        tsb = self.sb(f"{tag}tsb", [P, F], I32, scope)
        yy = self.sb(f"{tag}y", [P, F], I32, scope)
        pl = [self.sb(f"{tag}pl{i}", [P, F], BF16, scope) for i in range(2)]
        ssc = [self.sb(f"{tag}ss{i}", [8, F], F32, scope) for i in range(2)]
        pss = [self.psum(f"{tag}ps{i}", [P, 512], F32, scope) for i in range(2)]
        ld = self.sem(f"{tag}ld")
        lsc = self.sem(f"{tag}lsc")
        dec = self.sem(f"{tag}dec")
        rmm = self.sem(f"{tag}rmm")
        ptt = self.sem(f"{tag}ptt")
        sos = self.sem(f"{tag}so")
        nCB = CW // F
        blk = 0
        for k in range(JC):
            for cb in range(nCB):
                b = blk % 2
                c0 = cb * F
                if blk >= 2:
                    g.wait_ge(dec.h, blk - 1)
                    s.wait_ge(rmm.h, 4 * (blk - 1))
                wl = ld.inc(g.dma_start(out=bi[b][:],
                                        in_=src[P * k:P * (k + 1), c0:c0 + F]), 16)
                wsc = lsc.inc(s.dma_start(out=ssc[b][:],
                                          in_=scalesT[8 * k:8 * k + 8, c0:c0 + F]), 16)
                v.wait_ge(ld.h, wl)
                if blk >= 1:
                    v.wait_ge(sos.h, 16 * 32 * blk)
                v.tensor_scalar(tp[:], bi[b][:], 7, 6, ALU.bitwise_and, ALU.logical_shift_left)
                v.tensor_scalar(tq[:], bi[b][:], 0x70, 18, ALU.bitwise_and, ALU.logical_shift_left)
                v.tensor_tensor(ty[:], tp[:], tq[:], ALU.bitwise_or)
                v.tensor_scalar(tp[:], bi[b][:], 8, 12, ALU.bitwise_and, ALU.logical_shift_left)
                v.tensor_scalar(tq[:], bi[b][:], 0x80, 24, ALU.bitwise_and, ALU.logical_shift_left)
                v.tensor_tensor(tsb[:], tp[:], tq[:], ALU.bitwise_or)
                wd = dec.inc(v.tensor_tensor(yy[:], ty[:], tsb[:], ALU.bitwise_or), 1)
                ybf = yy[:].bitcast(BF16)
                t.wait_ge(lsc.h, wsc)
                for sub in range(F // 512):
                    pb = sub % 2
                    if blk * 4 + sub >= 2:
                        t.wait_ge(ptt.h, 2 * (blk * 4 + sub - 1))
                    wr = rmm.inc(t.matmul(pss[pb][:], self.repl_s[:],
                                          ssc[b][:, 512 * sub:512 * (sub + 1)],
                                          start=True, stop=True), 1)
                    v.wait_ge(rmm.h, wr)
                    lo = ybf[:, 2 * 512 * sub: 2 * 512 * (sub + 1): 2]
                    hi = ybf[:, 2 * 512 * sub + 1: 2 * 512 * (sub + 1): 2]
                    v.tensor_tensor(pl[0][:, 512 * sub:512 * (sub + 1)], lo, pss[pb][:], ALU.mult)
                    ptt.inc(v.tensor_tensor(pl[1][:, 512 * sub:512 * (sub + 1)], hi, pss[pb][:], ALU.mult), 2)
                s.wait_ge(ptt.h, ptt.n)
                for j in range(F // P):
                    cc0, b0 = hc_of(k, 0, c0 // P + j)
                    cc1, b1 = hc_of(k, 1, c0 // P + j)
                    sos.inc(s.dma_start(out=dest[cc0, b0], in_=pl[0][:, P * j:P * (j + 1)]), 16)
                    sos.inc(s.dma_start(out=dest[cc1, b1], in_=pl[1][:, P * j:P * (j + 1)]), 16)
                blk += 1
        return sos

    def ph4_dequant(self, scope):
        nc = self.nc
        engs = (nc.sync, nc.gpsimd, nc.vector, nc.scalar, nc.tensor)

        def barrier(sems):
            for eng in engs:
                for sem_ in sems:
                    eng.wait_ge(sem_.h, sem_.n)

        barrier([self.ph3_done, self.ph3_q_done, self.ph3_k_done])
        with ExitStack() as m1:
            d1 = self._dequant_mat(m1, "dqg", self.gWT, self.gST, 8, MLP, self.d_Wg,
                                   lambda k, plane, cc: (cc, k + 8 * plane))
        barrier([d1])
        with ExitStack() as m2:
            d2 = self._dequant_mat(m2, "dqu", self.uWT, self.uST, 8, MLP, self.d_Wu,
                                   lambda k, plane, cc: (cc, k + 8 * plane))
        barrier([d2])
        with ExitStack() as m3:
            d3 = self._dequant_mat(m3, "dqd", self.dWT, self.dST, 64, C, self.d_Wd,
                                   lambda k, plane, hh: (k + 64 * plane, hh))
        self.ph4_done = d3
        self.ph4_dones = [d1, d2, d3]

    def ph5_attention(self, scope):
        nc = self.nc
        s, g, v, sc, t = nc.sync, nc.gpsimd, nc.vector, nc.scalar, nc.tensor
        for eng in (s, g, v, sc, t):
            for sem_ in self.ph4_dones + [self.ph3_done, self.ph3_k_done, self.ph3_q_done]:
                eng.wait_ge(sem_.h, sem_.n)
        Ks = [self.sb(f"p5k{i}", [P, T], BF16, scope) for i in range(2)]
        Vs = [self.sb(f"p5v{i}", [P, T], BF16, scope) for i in range(2)]
        Qs = [self.sb(f"p5q{i}", [P, TOK], BF16, scope) for i in range(2)]
        Eb = [self.sb(f"p5e{i}", [P, TOK], BF16, scope) for i in range(2)]
        sums_sb = self.sb("p5sum", [1, TOK], F32, scope)
        recip = self.sb("p5rcp", [1, TOK], F32, scope)
        bc_sb = self.sb("p5bc", [P, TOK], F32, scope)
        Oo = [self.sb(f"p5o{i}", [P, TOK], BF16, scope) for i in range(2)]
        ps_S = self.psum("p5psS", [P, TOK], F32, scope)
        ps_O = self.psum("p5psO", [P, TOK], F32, scope)
        ps_s0 = self.psum("p5pss0", [1, 512], F32, scope)
        ps_s1 = self.psum("p5pss1", [1, 512], F32, scope)
        ps_bc = self.psum("p5psbc", [P, 512], F32, scope)
        ld = self.sem("p5ld")
        smm = self.sem("p5smm")
        sexp = self.sem("p5sexp")
        svm = self.sem("p5svm")
        spv = self.sem("p5spv")
        ssum = self.sem("p5ssum")
        snrm = self.sem("p5snrm")
        sso = self.sem("p5sso")
        ssb = self.sem("p5ssb")
        nkt_tot = 0
        for h in range(NH):
            hb = h % 2
            if h >= 2:
                s.wait_ge(snrm.h, h - 1)
            wl = ld.inc(s.dma_start(out=Ks[hb][:], in_=self.d_K[P * h:P * (h + 1), :]), 16)
            for ktl in range(T // P):
                wl = ld.inc(s.dma_start(
                    out=Vs[hb][:, P * ktl:P * (ktl + 1)],
                    in_=self.d_V[P * ktl:P * (ktl + 1), P * h:P * (h + 1)]), 16)
            wl = ld.inc(s.dma_start(out=Qs[hb][:], in_=self.d_Q[P * h:P * (h + 1), :]), 16)
            t.wait_ge(ld.h, wl)
            if h >= 1:
                t.wait_ge(snrm.h, h)
            base_exp = sexp.n
            base_pv = spv.n
            for kt in range(T // P):
                smin = kt // 8
                q0 = P * smin
                o = kt % 8
                pieces = ([(q0, 512), (512, TOK)] if smin < 4 else [(q0, TOK)])
                if kt > 0:
                    t.wait_ge(sexp.h, base_exp + kt)
                for pi_, (a, bnd) in enumerate(pieces):
                    mm = t.matmul(ps_S[:, a:bnd], Ks[hb][:, P * kt:P * (kt + 1)],
                                  Qs[hb][:, a:bnd], start=True, stop=True)
                wm = smm.inc(mm, 1)
                sc.wait_ge(smm.h, wm)
                if kt >= 2:
                    sc.wait_ge(spv.h, base_pv + kt - 1)
                we = sexp.inc(sc.activation(Eb[kt % 2][:, q0:TOK], ps_S[:, q0:TOK],
                                            AF.Exp, scale=SCALE, bias=EXP_SHIFT), 1)
                v.wait_ge(sexp.h, we)
                wv_ = svm.inc(v.tensor_tensor(Eb[kt % 2][:, q0:q0 + P],
                                              Eb[kt % 2][:, q0:q0 + P],
                                              self.masks_s[:, P * o:P * (o + 1)],
                                              ALU.mult), 1)
                t.wait_ge(svm.h, wv_)
                if smin < 4:
                    segs = [(q0, 512, ps_s0, 0, 39), (512, TOK, ps_s1, 512, 63)]
                else:
                    segs = [(q0, TOK, ps_s1, 512, 63)]
                for (a, bnd, pss_, sb0, lastkt) in segs:
                    mms = t.matmul(pss_[:, a - sb0:bnd - sb0], self.ones_bf[:],
                                   Eb[kt % 2][:, a:bnd],
                                   start=(kt == 0), stop=(kt == lastkt))
                    if kt == 63:
                        ssb.inc(mms, 1)
                    mm = t.matmul(ps_O[:, a:bnd], Vs[hb][:, P * kt:P * (kt + 1)],
                                  Eb[kt % 2][:, a:bnd],
                                  start=(kt == 0), stop=(kt == lastkt))
                spv.inc(mm, 1)
            v.wait_ge(spv.h, spv.n)
            v.wait_ge(ssb.h, ssb.n)
            v.wait_ge(smm.h, smm.n)
            v.tensor_copy(sums_sb[:, 0:512], ps_s0[:])
            v.tensor_copy(sums_sb[:, 512:TOK], ps_s1[:])
            wv_ = ssum.inc(v.reciprocal(recip[:], sums_sb[:]), 1)
            t.wait_ge(ssum.h, wv_)
            wbc = smm.inc(t.matmul(ps_bc[:], self.ones_r32[:], recip[:, 0:512],
                                   start=True, stop=True), 1)
            v.wait_ge(smm.h, wbc)
            wv_ = ssum.inc(v.tensor_copy(bc_sb[:, 0:512], ps_bc[:]), 1)
            t.wait_ge(ssum.h, wv_)
            wbc = smm.inc(t.matmul(ps_bc[:], self.ones_r32[:], recip[:, 512:TOK],
                                   start=True, stop=True), 1)
            v.wait_ge(smm.h, wbc)
            ssum.inc(v.tensor_copy(bc_sb[:, 512:TOK], ps_bc[:]), 1)
            v.tensor_tensor(Oo[hb][:, 0:512], ps_O[:, 0:512], bc_sb[:, 0:512], ALU.mult)
            wn = snrm.inc(v.tensor_tensor(Oo[hb][:, 512:TOK], ps_O[:, 512:TOK],
                                          bc_sb[:, 512:TOK], ALU.mult), 1)
            s.wait_ge(snrm.h, wn)
            sso.inc(s.dma_start(out=self.d_O[P * h:P * (h + 1), :], in_=Oo[hb][:]), 16)
        self.ph5_done = sso

    def ph6_proj_ln2(self, scope):
        nc = self.nc
        s, g, v, sc, t = nc.sync, nc.gpsimd, nc.vector, nc.scalar, nc.tensor
        for eng in (s, g, v, sc, t):
            eng.wait_ge(self.ph5_done.h, self.ph5_done.n)
        x1 = self.sb("p6x1", [P, HC * TOK], F32, scope)
        with ExitStack() as pa:
            wp = self.sb("p6wp", [P, HC * C], BF16, pa)
            Os = self.sb("p6o", [P, HC * TOK], BF16, pa)
            xc = [self.sb(f"p6xc{i}", [P, 512], F32, pa) for i in range(2)]
            pps = [self.psum(f"p6ps{i}", [P, 512], F32, pa) for i in range(2)]
            ldw = self.sem("p6ldw")
            ldx = self.sem("p6ldx")
            mme = self.sem("p6mm")
            evs = self.sem("p6ev")
            sxo = self.sem("p6so")
            for hc in range(HC):
                ldw.inc(g.dma_start(out=wp[:, hc * C:(hc + 1) * C],
                                    in_=self.w_proj_p[P * hc:P * (hc + 1), :]), 16)
            for chl in range(HC):
                ldw.inc(s.dma_start(
                    out=Os[:, TOK * chl:TOK * (chl + 1)],
                    in_=self.d_O[P * chl:P * (chl + 1), :]), 16)
            t.wait_ge(ldw.h, ldw.n)
            grp = 0
            for oc in range(HC):
                for half in range(2):
                    pb = grp % 2
                    wlx = ldx.inc(s.dma_start(
                        out=xc[pb][:],
                        in_=self.d_xfm[P * oc:P * (oc + 1),
                                       512 * half:512 * (half + 1)]), 16)
                    if grp >= 2:
                        t.wait_ge(evs.h, grp - 1)
                    for hc in range(HC):
                        mm = t.matmul(pps[pb][:],
                                      wp[:, hc * C + P * oc: hc * C + P * (oc + 1)],
                                      Os[:, hc * TOK + 512 * half: hc * TOK + 512 * (half + 1)],
                                      start=(hc == 0), stop=(hc == HC - 1))
                    wm = mme.inc(mm, 1)
                    v.wait_ge(mme.h, wm)
                    v.wait_ge(ldx.h, wlx)
                    col = oc * TOK + 512 * half
                    we = evs.inc(v.tensor_tensor(x1[:, col:col + 512], pps[pb][:],
                                                 xc[pb][:], ALU.add), 1)
                    s.wait_ge(evs.h, we)
                    sxo.inc(s.dma_start(out=self.d_x1[P * oc:P * (oc + 1),
                                                      512 * half:512 * (half + 1)],
                                        in_=x1[:, col:col + 512]), 16)
                    grp += 1
            self.ph6_x1_done = sxo
        for eng in (s, g, v, sc, t):
            eng.wait_ge(self.ph6_x1_done.h, self.ph6_x1_done.n)
        with ExitStack() as pb_:
            sq6 = self.sb("p6sq", [P, 512], F32, pb_)
            strow_t = self.sb("p6st", [1, 8 * 512], F32, pb_)
            def strow(i):
                return strow_t[0:1, 512 * i:512 * (i + 1)]
            rstd_b = self.sb("p6rb", [P, 512], F32, pb_)
            nmr_b = self.sb("p6nb", [P, 512], F32, pb_)
            tmp6 = self.sb("p6tmp", [P, 512], F32, pb_)
            l2c = self.sb("p6l2", [P, 512], BF16, pb_)
            ps_m1 = self.psum("p6pm1", [1, 512], F32, pb_)
            ps_m2 = self.psum("p6pm2", [1, 512], F32, pb_)
            ps_bc = self.psum("p6pbc", [P, 512], F32, pb_)
            sact = self.sem("p6act")
            scm = self.sem("p6cm")
            spl = self.sem("p6pl")
            smb = self.sem("p6mb")
            sm1 = self.sem("p6m1")
            sso2 = self.sem("p6so2")
            for half in range(2):
                if half == 1:
                    t.wait_ge(scm.h, scm.n)
                for oc in range(HC):
                    col = oc * TOK + 512 * half
                    if half > 0 or oc > 0:
                        sc.wait_ge(smb.h, smb.n)
                    wa = sact.inc(sc.activation(sq6[:], x1[:, col:col + 512], AF.Square), 1)
                    t.wait_ge(sact.h, wa)
                    mm1 = t.matmul(ps_m1[:], self.ones_c32[:], x1[:, col:col + 512],
                                   start=(oc == 0), stop=(oc == HC - 1))
                    if oc == HC - 1:
                        sm1.inc(mm1, 1)
                    wm = smb.inc(t.matmul(ps_m2[:], self.ones_c32[:], sq6[:],
                                          start=(oc == 0), stop=(oc == HC - 1)), 1)
                v.wait_ge(smb.h, smb.n)
                v.wait_ge(sm1.h, sm1.n)
                v.tensor_copy(strow(0), ps_m1[:])
                wv_ = scm.inc(v.tensor_copy(strow(1), ps_m2[:]), 1)
                g.wait_ge(scm.h, wv_)
                g.tensor_scalar(strow(2), strow(0), 1.0 / C, None, ALU.mult)
                g.tensor_scalar(strow(3), strow(1), 1.0 / C, None, ALU.mult)
                g.tensor_tensor(strow(4), strow(2), strow(2), ALU.mult)
                g.tensor_tensor(strow(3), strow(3), strow(4), ALU.subtract)
                wp_ = spl.inc(g.tensor_scalar(strow(3), strow(3), EPS, None, ALU.add), 1)
                sc.wait_ge(spl.h, wp_)
                wa = sact.inc(sc.activation(strow(4), strow(3), AF.Ln), 1)
                sc.wait_ge(sact.h, wa)
                wa = sact.inc(sc.activation(strow(4), strow(4), AF.Exp, scale=-0.5), 1)
                g.wait_ge(sact.h, wa)
                g.tensor_tensor(strow(5), strow(2), strow(4), ALU.mult)
                wp_ = spl.inc(g.tensor_scalar(strow(5), strow(5), -1.0, None, ALU.mult), 1)
                t.wait_ge(spl.h, wp_)
                wm = smb.inc(t.matmul(ps_bc[:], self.ones_r32[:], strow(4),
                                      start=True, stop=True), 1)
                v.wait_ge(smb.h, wm)
                wv_ = scm.inc(v.tensor_copy(rstd_b[:], ps_bc[:]), 1)
                t.wait_ge(scm.h, wv_)
                wm = smb.inc(t.matmul(ps_bc[:], self.ones_r32[:], strow(5),
                                      start=True, stop=True), 1)
                v.wait_ge(smb.h, wm)
                scm.inc(v.tensor_copy(nmr_b[:], ps_bc[:]), 1)
                for oc in range(HC):
                    col = oc * TOK + 512 * half
                    if half > 0 or oc > 0:
                        v.wait_ge(sso2.h, sso2.n)
                    v.tensor_tensor(tmp6[:], x1[:, col:col + 512], rstd_b[:], ALU.mult)
                    wv_ = scm.inc(v.tensor_tensor(l2c[:], tmp6[:], nmr_b[:], ALU.add), 1)
                    s.wait_ge(scm.h, wv_)
                    sso2.inc(s.dma_start(out=self.d_ln2[P * oc:P * (oc + 1),
                                                        512 * half:512 * (half + 1)],
                                         in_=l2c[:]), 16)
            self.ph6_done = sso2

    def ph7_ffn(self, scope):
        nc = self.nc
        s, g, v, sc, t = nc.sync, nc.gpsimd, nc.vector, nc.scalar, nc.tensor
        engs = (s, g, v, sc, t)

        def barrier(sems):
            for eng in engs:
                for sem_ in sems:
                    eng.wait_ge(sem_.h, sem_.n)

        barrier([self.ph6_done, self.ph6_x1_done] + self.ph4_dones)
        yso = self.sem("p7yso")
        for half in range(2):
            if half == 1:
                barrier([yso])
            cols = 512 * half
            with ExitStack() as hs:
                inter = self.sb(f"p7int{half}", [P, NCC * 512], BF16, hs)
                sA = self.sem(f"p7sA{half}")
                with ExitStack() as pa:
                    ln2s = self.sb(f"p7l2{half}", [P, HC * 512], BF16, pa)
                    wgb = [self.sb(f"p7wg{half}_{i}", [P, HC * P], BF16, pa) for i in range(2)]
                    wub = [self.sb(f"p7wu{half}_{i}", [P, HC * P], BF16, pa) for i in range(2)]
                    gel = self.sb(f"p7gel{half}", [P, 512], BF16, pa)
                    ps_g = [self.psum(f"p7pg{half}_{i}", [P, 512], F32, pa) for i in range(2)]
                    ps_u = [self.psum(f"p7pu{half}_{i}", [P, 512], F32, pa) for i in range(2)]
                    ld = self.sem(f"p7ld{half}")
                    smm = self.sem(f"p7mm{half}")
                    sgel = self.sem(f"p7gl{half}")
                    for hcl in range(HC):
                        wl = ld.inc(s.dma_start(
                            out=ln2s[:, 512 * hcl:512 * (hcl + 1)],
                            in_=self.d_ln2[P * hcl:P * (hcl + 1),
                                           cols:cols + 512]), 16)
                    t.wait_ge(ld.h, wl)
                    for cc in range(NCC):
                        b = cc % 2
                        if cc >= 2:
                            g.wait_ge(smm.h, 2 * (cc - 1))
                        wl = ld.inc(g.dma_start(
                            out=wgb[b][:].rearrange("p (h f) -> p h f", f=P),
                            in_=self.d_Wg[cc].rearrange("h p f -> p h f")), 16)
                        wl = ld.inc(g.dma_start(
                            out=wub[b][:].rearrange("p (h f) -> p h f", f=P),
                            in_=self.d_Wu[cc].rearrange("h p f -> p h f")), 16)
                        t.wait_ge(ld.h, wl)
                        if cc >= 2:
                            t.wait_ge(sA.h, cc - 1)
                        for hc in range(HC):
                            gmm = t.matmul(ps_g[b][:], wgb[b][:, P * hc:P * (hc + 1)],
                                           ln2s[:, 512 * hc:512 * (hc + 1)],
                                           start=(hc == 0), stop=(hc == HC - 1))
                        wm = smm.inc(gmm, 1)
                        for hc in range(HC):
                            mm = t.matmul(ps_u[b][:], wub[b][:, P * hc:P * (hc + 1)],
                                          ln2s[:, 512 * hc:512 * (hc + 1)],
                                          start=(hc == 0), stop=(hc == HC - 1))
                        wm = smm.inc(mm, 1)
                        sc.wait_ge(smm.h, wm)
                        if cc >= 1:
                            sc.wait_ge(sA.h, cc)
                        wg_ = sgel.inc(sc.activation(gel[:], ps_g[b][:],
                                                     AF.Gelu_apprx_tanh), 1)
                        v.wait_ge(sgel.h, wg_)
                        sA.inc(v.tensor_tensor(inter[:, 512 * cc:512 * (cc + 1)],
                                               gel[:], ps_u[b][:], ALU.mult), 1)
                barrier([sA])
                if self.phase == 8 and half == 1:
                    dbs = self.sem("p7dbg1")
                    dbs.inc(s.dma_start(out=self.d_dbE[:, :],
                                        in_=inter[:, 56 * 512:56 * 512 + C]), 16)
                    dbs.inc(s.dma_start(out=self.d_dbi[:, :], in_=inter[:, :]), 16)
                    self.dbg1 = dbs
                out_s = self.sb(f"p7out{half}", [P, HC * 512], F32, hs)
                with ExitStack() as pb_:
                    x1c = [self.sb(f"p7x1{half}_{i}", [P, 512], F32, pb_) for i in range(2)]
                    wdb = [self.sb(f"p7wd{half}_{i}", [P, 8 * P], BF16, pb_) for i in range(2)]
                    ps_d = [self.psum(f"p7pd{half}_{i}", [P, 512], F32, pb_) for i in range(8)]
                    ldd = self.sem(f"p7ldd{half}")
                    smd = self.sem(f"p7md{half}")
                    sev = self.sem(f"p7ev{half}")
                    ldx = self.sem(f"p7lx{half}")
                    NIP = NCC + 7
                    for pass_ in range(2):
                        if pass_ == 1:
                            t.wait_ge(sev.h, 8)
                        for cc in range(NCC):
                            b = cc % 2
                            gi = pass_ * NCC + cc
                            if gi >= 2:
                                pp, cp = (gi - 2) // NCC, (gi - 2) % NCC
                                req = NIP * pp + (cp + 1 if cp < NCC - 1 else NIP)
                                g.wait_ge(smd.h, req)
                            for jl in range(8):
                                wl = ldd.inc(g.dma_start(
                                    out=wdb[b][:, P * jl:P * (jl + 1)],
                                    in_=self.d_Wd[cc, 8 * pass_ + jl]), 16)
                            t.wait_ge(ldd.h, wl)
                            for j in range(8):
                                mm = t.matmul(ps_d[j][:], wdb[b][:, P * j:P * (j + 1)],
                                              inter[:, 512 * cc:512 * (cc + 1)],
                                              start=(cc == 0), stop=(cc == NCC - 1))
                                if cc == NCC - 1:
                                    smd.inc(mm, 1)
                            if cc < NCC - 1:
                                smd.inc(mm, 1)
                        for j in range(8):
                            hh = 8 * pass_ + j
                            xb = j % 2
                            if 8 * pass_ + j >= 2:
                                s.wait_ge(sev.h, 8 * pass_ + j - 1)
                            wlx = ldx.inc(s.dma_start(
                                out=x1c[xb][:],
                                in_=self.d_x1[P * hh:P * (hh + 1), cols:cols + 512]), 16)
                            v.wait_ge(smd.h, NIP * (pass_ + 1))
                            v.wait_ge(ldx.h, wlx)
                            sev.inc(v.tensor_tensor(out_s[:, 512 * hh:512 * (hh + 1)],
                                                    ps_d[j][:], x1c[xb][:], ALU.add), 1)
                if self.phase == 8 and half == 1:
                    self.dbg1.inc(s.dma_start(out=self.d_dbg[P:2 * P, :], in_=out_s[:, 0:C]), 16)
                    self.dbg1.inc(s.dma_start(out=self.d_dbL[:, :],
                                              in_=inter[:, 56 * 512:56 * 512 + C]), 16)
                if self.phase >= 9:
                    if half == 0:
                        self.dbg9 = self.sem("p7dbg9")
                    for eng in (s,):
                        eng.wait_ge(sev.h, sev.n)
                    self.dbg9.inc(s.dma_start(out=self.d_dbo[P * half:P * (half + 1), :],
                                              in_=out_s[:, :]), 16)
                barrier([sev])
                if self.phase >= 9 and half == 1:
                    self.dbg9.inc(s.dma_start(out=self.d_dbi[:, :], in_=inter[:, :]), 16)
                with ExitStack() as pc_:
                    asm = self.sb(f"p7asm{half}", [P, C], F32, pc_)
                    ps_t = self.psum(f"p7pt{half}", [P, P], F32, pc_)
                    stp = self.sem(f"p7tp{half}")
                    scp = self.sem(f"p7cp{half}")
                    for t4 in range(4):
                        if t4 > 0:
                            t.wait_ge(yso.h, yso.n)
                        for hh in range(HC):
                            if hh > 0 or t4 > 0:
                                t.wait_ge(scp.h, scp.n)
                            wt = stp.inc(t.transpose(
                                ps_t[:], out_s[:, 512 * hh + P * t4: 512 * hh + P * (t4 + 1)],
                                self.ident3_s[:]), 1)
                            v.wait_ge(stp.h, wt)
                            if hh < 8:
                                dstc = asm[:, 256 * hh: 256 * (hh + 1): 2]
                            else:
                                dstc = asm[:, 256 * (hh - 8) + 1: 256 * (hh - 7): 2]
                            scp.inc(v.tensor_copy(dstc, ps_t[:]), 1)
                        s.wait_ge(scp.h, scp.n)
                        yso.inc(s.dma_start(
                            out=self.y[512 * half + P * t4: 512 * half + P * (t4 + 1), :],
                            in_=asm[:]), 16)
                        if self.phase == 8 and half == 1 and t4 == 0:
                            self.dbg1.inc(s.dma_start(out=self.d_dbg[3 * P:4 * P, :],
                                                      in_=asm[:]), 16)
        self.ph7_done = yso

    def epilogue(self, scope):
        nc = self.nc
        s, v = nc.sync, nc.vector
        fin = self.sem("fin")
        cv = self.sem("dbgcv")
        ld = self.sem("dbgld")

        def dump_dram_bf16(row, src_ap, w=C):
            dbge = self.sb(f"dbge{row}", [P, w], BF16, scope)
            dbgf = self.sb(f"dbgf{row}", [P, w], F32, scope)
            wl = ld.inc(s.dma_start(out=dbge[:], in_=src_ap), 16)
            v.wait_ge(ld.h, wl)
            wc = cv.inc(v.tensor_copy(dbgf[:], dbge[:]), 1)
            s.wait_ge(cv.h, wc)
            fin.inc(s.dma_start(out=self.y[row * P:(row + 1) * P, 0:w], in_=dbgf[:]), 16)

        def dump_dram_f32(row, src_ap, w=C):
            dbgf = self.sb(f"dbgg{row}", [P, w], F32, scope)
            wl = ld.inc(s.dma_start(out=dbgf[:], in_=src_ap), 16)
            s.wait_ge(ld.h, wl)
            fin.inc(s.dma_start(out=self.y[row * P:(row + 1) * P, 0:w], in_=dbgf[:]), 16)

        if self.phase >= 7:
            s.wait_ge(self.ph7_done.h, self.ph7_done.n)
            if self.phase == 8:
                s.wait_ge(self.dbg1.h, self.dbg1.n)
                dump_dram_bf16(0, self.d_dbE[:, :])
                dump_dram_f32(1, self.d_dbg[P:2 * P, :])
                dump_dram_bf16(2, self.d_dbL[:, :])
                dump_dram_f32(3, self.d_dbg[3 * P:4 * P, :])
        elif self.phase >= 6:
            s.wait_ge(self.ph6_done.h, self.ph6_done.n)
            s.wait_ge(self.ph6_x1_done.h, self.ph6_x1_done.n)
            dump_dram_f32(0, self.d_x1[0:P, :], TOK)
            dump_dram_bf16(1, self.d_ln2[0:P, :], TOK)
            dump_dram_f32(2, self.d_x1[5 * P:6 * P, :], TOK)
        elif self.phase >= 5:
            s.wait_ge(self.ph5_done.h, self.ph5_done.n)
            dump_dram_bf16(0, self.d_O[0:P, :], TOK)
            dump_dram_bf16(1, self.d_O[P:2 * P, :], TOK)
        elif self.phase >= 4:
            for sem_ in self.ph4_dones:
                s.wait_ge(sem_.h, sem_.n)

            def dump_blocks(row, dest, blk):
                dbge = self.sb(f"dbge{row}", [P, C], BF16, scope)
                dbgf = self.sb(f"dbgf{row}", [P, C], F32, scope)
                wl = ld.n
                for cc in range(16):
                    wl = ld.inc(s.dma_start(out=dbge[:, P * cc:P * (cc + 1)],
                                            in_=dest[cc, blk]), 16)
                v.wait_ge(ld.h, wl)
                wc = cv.inc(v.tensor_copy(dbgf[:], dbge[:]), 1)
                s.wait_ge(cv.h, wc)
                fin.inc(s.dma_start(out=self.y[row * P:(row + 1) * P, :], in_=dbgf[:]), 16)

            dump_blocks(0, self.d_Wg, 3)
            dump_blocks(1, self.d_Wu, 11)
            dump_blocks(2, self.d_Wd, 5)
        elif self.phase >= 3:
            for sem_ in (self.ph3_done, self.ph3_k_done, self.ph3_q_done):
                s.wait_ge(sem_.h, sem_.n)
            dump_dram_bf16(0, self.d_K[0:P, 0:C])
            dump_dram_bf16(1, self.d_V[0:P, 0:C])
            dump_dram_bf16(2, self.d_Q[0:P, :], TOK)
        elif self.phase >= 2:
            s.wait_ge(self.ph2_done.h, self.ph2_done.n)
            dump_dram_bf16(0, self.d_ln1fmT[0:P, 0:C])
        else:
            s.wait_ge(self.ph1_full_done.h, self.ph1_full_done.n)
            s.wait_ge(self.ph1_own_done.h, self.ph1_own_done.n)
            dump_dram_bf16(0, self.d_ln1tm[0:P, :])
            dump_dram_bf16(1, self.d_lo[0:P, :], TOK)
            dump_dram_f32(2, self.d_xfm[0:P, :], TOK)
        s.wait_ge(fin.h, fin.n)


def kernel(**inputs):
    b = Builder(phase=7)
    nc = b.build()
    in_maps = [host_prep(inputs, core) for core in range(8)]
    from concourse.bass_utils import run_bass_kernel_spmd
    res = run_bass_kernel_spmd(nc, in_maps, core_ids=list(range(8)))
    out = np.empty((T, C), np.float32)
    for core in range(8):
        out[own_rows(core)] = np.asarray(res.results[core]["y"], np.float32)
    return out.reshape(1, T, C)


# version 03
# speedup vs baseline: 1.3204x; 1.3204x over previous
import numpy as np
import ml_dtypes
import concourse.bass as bass
import concourse.mybir as mybir
from contextlib import ExitStack

dt = mybir.dt
AF = mybir.ActivationFunctionType
ALU = mybir.AluOpType
AX = mybir.AxisListType.X
F32, BF16, I32, U8 = dt.float32, dt.bfloat16, dt.int32, dt.uint8
BF = ml_dtypes.bfloat16

P = 128
T = 8192
C = 2048
NH = 16
D = 128
NS = 8
TOK = NS * P
MLP = 16384
NCC = MLP // P
HC = C // P
SCALE = float(1.0 / np.sqrt(128.0))
EXP_SHIFT = -12.0
EPS = 1e-5

PI = np.concatenate([np.arange(0, C, 2), np.arange(1, C, 2)])
PIC = np.concatenate([np.arange(0, MLP, 2), np.arange(1, MLP, 2)])


def own_rows(core):
    return np.concatenate(
        [np.arange(P * (8 * s + core), P * (8 * s + core) + P) for s in range(NS)])


def host_prep(inputs, core):
    x = np.asarray(inputs["L0_attn_x"])[0]
    w_attn = np.asarray(inputs["L0_attn_w_attn"])
    w_proj = np.asarray(inputs["L0_attn_w_proj"])
    gW = np.asarray(inputs["L0_ffn_gate_W_packed"])
    gS = np.asarray(inputs["L0_ffn_gate_scales"])
    uW = np.asarray(inputs["L0_ffn_up_W_packed"])
    uS = np.asarray(inputs["L0_ffn_up_scales"])
    dW = np.asarray(inputs["L0_ffn_down_W_packed"])
    dS = np.asarray(inputs["L0_ffn_down_scales"])
    rows = own_rows(core)
    m = {}
    m["x_full"] = np.ascontiguousarray(x, dtype=np.float32)
    m["x_own"] = np.ascontiguousarray(x[rows], dtype=np.float32)
    m["w_attn_p"] = np.ascontiguousarray(w_attn[PI, :], dtype=np.float32)
    m["w_proj_p"] = np.ascontiguousarray(w_proj[:, PI], dtype=np.float32)
    m["gWT"] = np.ascontiguousarray(gW[PIC].astype(np.uint8).T)
    m["uWT"] = np.ascontiguousarray(uW[PIC].astype(np.uint8).T)
    m["dWT"] = np.ascontiguousarray(dW[PI].astype(np.uint8).T)
    m["gST"] = np.ascontiguousarray(gS[PIC].T, dtype=np.float32)
    m["uST"] = np.ascontiguousarray(uS[PIC].T, dtype=np.float32)
    m["dST"] = np.ascontiguousarray(dS[PI].T, dtype=np.float32)
    m["ident32"] = np.eye(P, dtype=np.float32)
    m["identbf"] = np.eye(P, dtype=BF)
    kk = np.arange(P)[None, :, None]
    qq = np.arange(P)[None, None, :]
    oo = np.arange(NS)[:, None, None]
    m["masks"] = (128 * oo + kk <= 128 * core + qq).astype(BF)
    R = np.zeros((8, P), np.float32)
    for g_ in range(8):
        R[g_, 16 * g_:16 * g_ + 16] = float(2.0 ** 126)
    m["repl"] = R
    return m


class Sem:
    __slots__ = ("h", "n")

    def __init__(self, h):
        self.h = h
        self.n = 0

    def inc(self, instr, amt=1):
        instr.then_inc(self.h, amt)
        self.n += amt
        return self.n


class Builder:
    def __init__(self, phase=99):
        self.phase = phase
        self.nc = bass.Bass("TRN2", target_bir_lowering=False)
        self.es = ExitStack()
        self.es.__enter__()
        self._semid = 0

    def sb(self, name, shape, dtype, scope=None):
        return (scope or self.es).enter_context(self.nc.sbuf_tensor(name, shape, dtype))

    def psum(self, name, shape, dtype=F32, scope=None):
        return (scope or self.es).enter_context(self.nc.psum_tensor(name, shape, dtype))

    def sem(self, name=None):
        self._semid += 1
        h = self.es.enter_context(self.nc.semaphore(name or f"s{self._semid}"))
        return Sem(h)

    def dram(self, name, shape, dtype, kind="Internal"):
        return self.nc.dram_tensor(name, shape, dtype, kind=kind).ap()

    def build(self):
        nc = self.nc
        self.x_full = self.dram("x_full", [T, C], F32, "ExternalInput")
        self.x_own = self.dram("x_own", [TOK, C], F32, "ExternalInput")
        self.w_attn_p = self.dram("w_attn_p", [C, 3 * C], F32, "ExternalInput")
        self.w_proj_p = self.dram("w_proj_p", [C, C], F32, "ExternalInput")
        self.gWT = self.dram("gWT", [C // 2, MLP], U8, "ExternalInput")
        self.uWT = self.dram("uWT", [C // 2, MLP], U8, "ExternalInput")
        self.dWT = self.dram("dWT", [MLP // 2, C], U8, "ExternalInput")
        self.gST = self.dram("gST", [64, MLP], F32, "ExternalInput")
        self.uST = self.dram("uST", [64, MLP], F32, "ExternalInput")
        self.dST = self.dram("dST", [512, C], F32, "ExternalInput")
        self.ident32 = self.dram("ident32", [P, P], F32, "ExternalInput")
        self.identbf = self.dram("identbf", [P, P], BF16, "ExternalInput")
        self.masks = self.dram("masks", [NS, P, P], BF16, "ExternalInput")
        self.repl = self.dram("repl", [8, P], F32, "ExternalInput")

        self.d_ln1tm = self.dram("d_ln1tm", [T, C], BF16)
        self.d_ln1fmT = self.dram("d_ln1fmT", [C, T], BF16)
        self.d_lo = self.dram("d_lo", [C, TOK], BF16)
        self.d_xfm = self.dram("d_xfm", [C, TOK], F32)
        self.d_K = self.dram("d_K", [C, T], BF16)
        self.d_V = self.dram("d_V", [T, C], BF16)
        self.d_Q = self.dram("d_Q", [C, TOK], BF16)
        self.d_O = self.dram("d_O", [C, TOK], BF16)
        self.d_x1 = self.dram("d_x1", [C, TOK], F32)
        self.d_ln2 = self.dram("d_ln2", [C, TOK], BF16)
        self.d_Wg = self.dram("d_Wg", [NCC, HC, P, P], BF16)
        self.d_Wu = self.dram("d_Wu", [NCC, HC, P, P], BF16)
        self.d_Wd = self.dram("d_Wd", [NCC, HC, P, P], BF16)
        self.d_dbg = self.dram("d_dbg", [4 * P, C], F32)
        self.d_dbE = self.dram("d_dbE", [P, C], BF16)
        self.d_dbi = self.dram("d_dbi", [P, NCC * 512], BF16)
        self.d_dbx = self.dram("d_dbx", [4 * P, TOK], F32)
        self.d_dbo = self.dram("d_dbo", [2 * P, HC * 512], F32)
        self.d_dbL = self.dram("d_dbL", [P, C], BF16)

        self.y = self.dram("y", [TOK, C], F32, "ExternalOutput")

        self.identb_s = self.sb("identb_s", [P, P], BF16)
        self.ident3_s = self.sb("ident3_s", [P, P], F32)
        self.ones_bf = self.sb("ones_bf", [P, 1], BF16)
        self.masks_s = self.sb("masks_s", [P, NS * P], BF16)
        self.repl_s = self.sb("repl_s", [8, P], F32)
        self.consts = self.sb("consts", [P, 2], F32)

        s, v = nc.sync, nc.vector
        ld0 = self.sem("ld0")
        ld0.inc(s.dma_start(out=self.identb_s[:], in_=self.identbf[:]), 16)
        ld0.inc(s.dma_start(out=self.ident3_s[:], in_=self.ident32[:]), 16)
        ld0.inc(s.dma_start(out=self.masks_s[:].rearrange("k (o q) -> k o q", q=P),
                            in_=self.masks[:].rearrange("o k q -> k o q")), 16)
        ld0.inc(s.dma_start(out=self.repl_s[:], in_=self.repl[:]), 16)
        self.ones_r32 = self.sb("ones_r32", [1, P], F32)
        self.ones_c32 = self.sb("ones_c32", [P, 1], F32)
        v.wait_ge(ld0.h, ld0.n)
        v.memset(self.ones_bf[:], 1.0)
        v.memset(self.ones_r32[:], 1.0)
        v.memset(self.ones_c32[:], 1.0)
        self.c_rdy = self.sem("c_rdy")
        v.memset(self.consts[:, 0:1], EXP_SHIFT)
        nc.const_aps.register(F32, EXP_SHIFT, self.consts[:, 0:1])
        self.c_rdy.inc(v.memset(self.consts[:, 1:2], SCALE), 1)
        nc.const_aps.register(F32, SCALE, self.consts[:, 1:2])
        self.ld0 = ld0

        with ExitStack() as sc1:
            self.ph1_ln1(sc1)
        if self.phase >= 2:
            with ExitStack() as sc2:
                self.ph2_bounce(sc2)
        if self.phase >= 3:
            with ExitStack() as sc3:
                self.ph3_kvq(sc3)
        if self.phase >= 4:
            with ExitStack() as sc4:
                self.ph4_dequant(sc4)
        if self.phase >= 5:
            with ExitStack() as sc5:
                self.ph5_attention(sc5)
        if self.phase >= 6:
            with ExitStack() as sc6:
                self.ph6_proj_ln2(sc6)
        if self.phase >= 7:
            with ExitStack() as sc7:
                self.ph7_ffn(sc7)
        with ExitStack() as sce:
            self.epilogue(sce)
        self.es.__exit__(None, None, None)
        return nc

    def _ln_tile(self, xin, lnout, st, sq, s_act, s_cm):
        nc = self.nc
        v, sc, pool = nc.vector, nc.scalar, nc.gpsimd
        wv = s_cm.inc(v.reduce_sum(st[:, 0:1], xin, axis=AX), 1)
        wa = s_act.inc(sc.activation(sq[:], xin, AF.Square, accum_out=st[:, 1:2]), 1)
        pool.wait_ge(s_cm.h, wv)
        pool.wait_ge(s_act.h, wa)
        pool.tensor_scalar(st[:, 2:3], st[:, 0:1], 1.0 / C, None, ALU.mult)
        pool.tensor_scalar(st[:, 3:4], st[:, 1:2], 1.0 / C, None, ALU.mult)
        pool.tensor_tensor(st[:, 4:5], st[:, 2:3], st[:, 2:3], ALU.mult)
        pool.tensor_tensor(st[:, 3:4], st[:, 3:4], st[:, 4:5], ALU.subtract)
        wp = s_cm.inc(pool.tensor_scalar(st[:, 3:4], st[:, 3:4], EPS, None, ALU.add), 1)
        sc.wait_ge(s_cm.h, wp)
        wa = s_act.inc(sc.activation(st[:, 5:6], st[:, 3:4], AF.Ln), 1)
        sc.wait_ge(s_act.h, wa)
        wa = s_act.inc(sc.activation(st[:, 5:6], st[:, 5:6], AF.Exp, scale=-0.5), 1)
        pool.wait_ge(s_act.h, wa)
        pool.tensor_tensor(st[:, 6:7], st[:, 2:3], st[:, 5:6], ALU.mult)
        pool.tensor_scalar(st[:, 6:7], st[:, 6:7], -1.0, None, ALU.mult)
        pool.tensor_scalar(lnout[:, 0:C // 2], xin[:, 0:C:2],
                           st[:, 5:6], st[:, 6:7], ALU.mult, ALU.add)
        return pool.tensor_scalar(lnout[:, C // 2:C], xin[:, 1:C:2],
                                  st[:, 5:6], st[:, 6:7], ALU.mult, ALU.add)

    def ph1_ln1(self, scope):
        nc = self.nc
        s, v, sc, g, t = nc.sync, nc.vector, nc.scalar, nc.gpsimd, nc.tensor
        NB = 3
        xt = [self.sb(f"p1x{i}", [P, C], F32, scope) for i in range(NB)]
        lt = [self.sb(f"p1l{i}", [P, C], BF16, scope) for i in range(NB)]
        sq = self.sb("p1sq", [P, C], F32, scope)
        stats = [self.sb(f"p1st{i}", [P, 8], F32, scope) for i in range(NB)]
        ld = self.sem("p1ld")
        act = self.sem("p1act")
        cm = self.sem("p1cm")
        ap = self.sem("p1ap")
        so = self.sem("p1so")
        ntile = T // P
        for i in range(ntile):
            b = i % NB
            if i >= NB:
                s.wait_ge(so.h, 16 * (i - NB + 1))
            wl = ld.inc(s.dma_start(out=xt[b][:], in_=self.x_full[P * i:P * (i + 1), :]), 16)
            v.wait_ge(ld.h, wl)
            sc.wait_ge(ld.h, wl)
            g.wait_ge(ld.h, wl)
            if i >= NB:
                v.wait_ge(ap.h, 2 * (i - NB + 1))
                g.wait_ge(so.h, 16 * (i - NB + 1))
            wp = ap.inc(self._ln_tile(xt[b][:], lt[b][:], stats[b], sq, act, cm), 2)
            s.wait_ge(ap.h, wp)
            so.inc(s.dma_start(out=self.d_ln1tm[P * i:P * (i + 1), :], in_=lt[b][:]), 16)
        self.ph1_full_done = so

        xo = [self.sb(f"p1ox{i}", [P, C], F32, scope) for i in range(2)]
        lo = [self.sb(f"p1ol{i}", [P, C], BF16, scope) for i in range(2)]
        sqo = self.sb("p1osq", [P, C], F32, scope)
        stato = [self.sb(f"p1ost{i}", [P, 8], F32, scope) for i in range(2)]
        evx = self.sb("p1oevx", [P, P], F32, scope)
        evl = self.sb("p1oevl", [P, P], BF16, scope)
        ldo = self.sem("p1old")
        acto = self.sem("p1oact")
        cmo = self.sem("p1ocm")
        apo = self.sem("p1oap")
        mmo = self.sem("p1omm")
        evo = self.sem("p1oev")
        soo = self.sem("p1oso")
        ps32 = self.psum("p1ops32", [P, P], F32, scope)
        psbf = self.psum("p1opsbf", [P, P], BF16, scope)
        for i in range(NS):
            b = i % 2
            if i >= 2:
                g.wait_ge(soo.h, 32 * HC * (i - 1))
            wl = ldo.inc(g.dma_start(out=xo[b][:], in_=self.x_own[P * i:P * (i + 1), :]), 16)
            v.wait_ge(ldo.h, wl)
            sc.wait_ge(ldo.h, wl)
            g.wait_ge(ldo.h, wl)
            if i >= 2:
                v.wait_ge(apo.h, 2 * (i - 1))
            wp = apo.inc(self._ln_tile(xo[b][:], lo[b][:], stato[b], sqo, acto, cmo), 2)
            t.wait_ge(ldo.h, wl)
            t.wait_ge(apo.h, wp)
            for hc in range(HC):
                if hc < 8:
                    cols = xo[b][:, 2 * P * hc: 2 * P * (hc + 1): 2]
                else:
                    cols = xo[b][:, 1 + 2 * P * (hc - 8): 2 * P * (hc - 7): 2]
                if i > 0 or hc > 0:
                    t.wait_ge(evo.h, evo.n)
                mmo.inc(t.transpose(ps32[:], cols, self.ident3_s[:]), 1)
                wm = mmo.inc(t.transpose(psbf[:], lo[b][:, P * hc:P * (hc + 1)],
                                         self.identb_s[:]), 1)
                v.wait_ge(mmo.h, wm)
                if i > 0 or hc > 0:
                    v.wait_ge(soo.h, soo.n)
                evo.inc(v.tensor_copy(evx[:], ps32[:]), 1)
                we = evo.inc(v.tensor_copy(evl[:], psbf[:]), 1)
                s.wait_ge(evo.h, we)
                soo.inc(s.dma_start(out=self.d_xfm[P * hc:P * (hc + 1),
                                                   P * i:P * (i + 1)], in_=evx[:]), 16)
                soo.inc(s.dma_start(out=self.d_lo[P * hc:P * (hc + 1),
                                                  P * i:P * (i + 1)], in_=evl[:]), 16)
        self.ph1_own_done = soo

    def ph2_bounce(self, scope):
        nc = self.nc
        s = nc.sync
        tb = [self.sb(f"p2t{i}", [P, T], BF16, scope) for i in range(2)]
        ld = self.sem("p2ld")
        so = self.sem("p2so")
        s.wait_ge(self.ph1_full_done.h, self.ph1_full_done.n)
        s.wait_ge(self.ph1_own_done.h, self.ph1_own_done.n)
        for k in range(HC):
            b = k % 2
            if k >= 2:
                s.wait_ge(so.h, 16 * (k - 1))
            for jt in range(16):
                wl = ld.inc(s.dma_start_transpose(
                    out=tb[b][:, 512 * jt:512 * (jt + 1)],
                    in_=self.d_ln1tm[512 * jt:512 * (jt + 1), P * k:P * (k + 1)]), 16)
            s.wait_ge(ld.h, wl)
            so.inc(s.dma_start(out=self.d_ln1fmT[P * k:P * (k + 1), :], in_=tb[b][:]), 16)
        self.ph2_done = so

    def _proj_pass(self, scope, tag, w_cols, out_mode, after=()):
        nc = self.nc
        s, g, v, sc, t = nc.sync, nc.gpsimd, nc.vector, nc.scalar, nc.tensor
        for eng in (s, g, v, sc, t):
            for sem_ in after:
                eng.wait_ge(sem_.h, sem_.n)
        W = 2048
        wsb = self.sb(f"{tag}w", [P, HC * W], BF16, scope)
        ldw = self.sem(f"{tag}ldw")
        for hc in range(HC):
            ldw.inc(g.dma_start(out=wsb[:, hc * W:(hc + 1) * W],
                                in_=self.w_attn_p[P * hc:P * (hc + 1),
                                                  w_cols[0]:w_cols[1]]), 16)
        rb = [self.sb(f"{tag}r{i}", [P, HC * 512], BF16, scope) for i in range(2)]
        ev = [self.sb(f"{tag}e{i}", [P, 512], BF16, scope) for i in range(4)]
        pss = [self.psum(f"{tag}p{i}", [P, 512], F32, scope) for i in range(4)]
        ld = self.sem(f"{tag}ld")
        mme = self.sem(f"{tag}mm")
        evs = self.sem(f"{tag}ev")
        sos = self.sem(f"{tag}so")
        t.wait_ge(ldw.h, ldw.n)
        grp = 0
        tr_mme = []
        NTR = T // 512
        for tr in range(NTR):
            b = tr % 2
            if tr >= 2:
                s.wait_ge(mme.h, tr_mme[tr - 2])
            for hcl in range(HC):
                wl = ld.inc(s.dma_start(
                    out=rb[b][:, 512 * hcl:512 * (hcl + 1)],
                    in_=self.d_ln1fmT[P * hcl:P * (hcl + 1),
                                      512 * tr:512 * (tr + 1)]), 16)
            t.wait_ge(ld.h, wl)
            if out_mode == "fm":
                sub = [("fm", kc) for kc in range(HC)]
            else:
                sub = [("tm", (tt, vc)) for tt in range(4) for vc in range(4)]
            for mode, idx in sub:
                pb = grp % 4
                if grp >= 4:
                    t.wait_ge(evs.h, grp - 3)
                if mode == "fm":
                    kc = idx
                    for hc in range(HC):
                        mm = t.matmul(pss[pb][:],
                                      wsb[:, hc * W + P * kc: hc * W + P * (kc + 1)],
                                      rb[b][:, hc * 512:(hc + 1) * 512],
                                      start=(hc == 0), stop=(hc == HC - 1))
                else:
                    tt, vc = idx
                    for hc in range(HC):
                        mm = t.matmul(pss[pb][:],
                                      rb[b][:, hc * 512 + P * tt: hc * 512 + P * (tt + 1)],
                                      wsb[:, hc * W + 512 * vc: hc * W + 512 * (vc + 1)],
                                      start=(hc == 0), stop=(hc == HC - 1))
                wm = mme.inc(mm, 1)
                eng = v if grp % 2 == 0 else sc
                eng.wait_ge(mme.h, wm)
                if grp >= 4:
                    eng.wait_ge(sos.h, 16 * (grp - 3))
                if eng is v:
                    we = evs.inc(v.tensor_copy(ev[pb][:], pss[pb][:]), 1)
                else:
                    we = evs.inc(sc.copy(ev[pb][:], pss[pb][:]), 1)
                s.wait_ge(evs.h, we)
                if mode == "fm":
                    dst = self.d_K[P * idx:P * (idx + 1), 512 * tr:512 * (tr + 1)]
                else:
                    tt, vc = idx
                    dst = self.d_V[512 * tr + P * tt: 512 * tr + P * (tt + 1),
                                   512 * vc:512 * (vc + 1)]
                sos.inc(s.dma_start(out=dst, in_=ev[pb][:]), 16)
                grp += 1
            tr_mme.append(mme.n)
        return sos

    def ph3_kvq(self, scope):
        nc = self.nc
        s, g, v, t = nc.sync, nc.gpsimd, nc.vector, nc.tensor
        with ExitStack() as kp:
            self.ph3_k_done = self._proj_pass(
                kp, "p3k", (C, 2 * C), "fm",
                after=(self.ph1_full_done, self.ph1_own_done, self.ph2_done))
        with ExitStack() as vp:
            self.ph3_done = self._proj_pass(vp, "p3v", (2 * C, 3 * C), "tm",
                                            after=(self.ph3_k_done,))
        with ExitStack() as qp:
            for eng in (s, g, v, t):
                eng.wait_ge(self.ph3_done.h, self.ph3_done.n)
            wq = self.sb("p3qw", [P, HC * C], BF16, qp)
            ldw = self.sem("p3qldw")
            for hc in range(HC):
                ldw.inc(g.dma_start(out=wq[:, hc * C:(hc + 1) * C],
                                    in_=self.w_attn_p[P * hc:P * (hc + 1), 0:C]), 16)
            lob = self.sb("p3qlo", [P, HC * 512], BF16, qp)
            ev = [self.sb(f"p3qe{i}", [P, 512], BF16, qp) for i in range(2)]
            pss = [self.psum(f"p3qp{i}", [P, 512], F32, qp) for i in range(2)]
            ld = self.sem("p3qld")
            mme = self.sem("p3qmm")
            evs = self.sem("p3qev")
            sos = self.sem("p3qso")
            t.wait_ge(ldw.h, ldw.n)
            g.wait_ge(self.ph1_own_done.h, self.ph1_own_done.n)
            grp = 0
            for half in range(2):
                if half == 1:
                    g.wait_ge(mme.h, mme.n)
                for hcl in range(HC):
                    wl = ld.inc(g.dma_start(
                        out=lob[:, 512 * hcl:512 * (hcl + 1)],
                        in_=self.d_lo[P * hcl:P * (hcl + 1),
                                      512 * half:512 * (half + 1)]), 16)
                t.wait_ge(ld.h, wl)
                for h in range(NH):
                    pb = grp % 2
                    if grp >= 2:
                        t.wait_ge(evs.h, grp - 1)
                    for hc in range(HC):
                        mm = t.matmul(pss[pb][:],
                                      wq[:, hc * C + P * h: hc * C + P * (h + 1)],
                                      lob[:, hc * 512:(hc + 1) * 512],
                                      start=(hc == 0), stop=(hc == HC - 1))
                    wm = mme.inc(mm, 1)
                    v.wait_ge(mme.h, wm)
                    if grp >= 2:
                        v.wait_ge(sos.h, 16 * (grp - 1))
                    we = evs.inc(v.tensor_copy(ev[pb][:], pss[pb][:]), 1)
                    s.wait_ge(evs.h, we)
                    sos.inc(s.dma_start(
                        out=self.d_Q[P * h:P * (h + 1), 512 * half:512 * (half + 1)],
                        in_=ev[pb][:]), 16)
                    grp += 1
            self.ph3_q_done = sos

    def _dequant_mat(self, scope, tag, src, scalesT, JC, CW, dest, hc_of):
        nc = self.nc
        s, g, v, t = nc.sync, nc.gpsimd, nc.vector, nc.tensor
        F = 2048
        bi = [self.sb(f"{tag}b{i}", [P, F], I32, scope) for i in range(2)]
        tp = self.sb(f"{tag}tp", [P, F], I32, scope)
        tq = self.sb(f"{tag}tq", [P, F], I32, scope)
        tp2 = self.sb(f"{tag}tp2", [P, F], I32, scope)
        tq2 = self.sb(f"{tag}tq2", [P, F], I32, scope)
        ty = self.sb(f"{tag}ty", [P, F], I32, scope)
        tsb = self.sb(f"{tag}tsb", [P, F], I32, scope)
        yy = self.sb(f"{tag}y", [P, F], I32, scope)
        pl = [[self.sb(f"{tag}pl{i}_{pn}", [P, F], BF16, scope) for pn in range(2)]
              for i in range(2)]
        ssc = [self.sb(f"{tag}ss{i}", [8, F], F32, scope) for i in range(2)]
        pss = [self.psum(f"{tag}ps{i}", [P, 512], F32, scope) for i in range(4)]
        ld = self.sem(f"{tag}ld")
        lsc = self.sem(f"{tag}lsc")
        dec = self.sem(f"{tag}dec")
        rmm = self.sem(f"{tag}rmm")
        ptt = self.sem(f"{tag}ptt")
        sos = self.sem(f"{tag}so")
        nCB = CW // F
        blk = 0
        for k in range(JC):
            for cb in range(nCB):
                b = blk % 2
                c0 = cb * F
                if blk >= 2:
                    g.wait_ge(dec.h, blk - 1)
                    g.wait_ge(rmm.h, 4 * (blk - 1))
                wl = ld.inc(g.dma_start(out=bi[b][:],
                                        in_=src[P * k:P * (k + 1), c0:c0 + F]), 16)
                wsc = lsc.inc(g.dma_start(out=ssc[b][:],
                                          in_=scalesT[8 * k:8 * k + 8, c0:c0 + F]), 16)
                t.wait_ge(lsc.h, wsc)
                for sub in range(F // 512):
                    if blk >= 1:
                        t.wait_ge(ptt.h, 2 * (4 * (blk - 1) + sub + 1))
                    rmm.inc(t.matmul(pss[sub][:], self.repl_s[:],
                                     ssc[b][:, 512 * sub:512 * (sub + 1)],
                                     start=True, stop=True), 1)
                v.wait_ge(ld.h, wl)
                if blk >= 2:
                    v.wait_ge(sos.h, 16 * 32 * (blk - 1))
                v.tensor_scalar(tp[:], bi[b][:], 7, 6, ALU.bitwise_and, ALU.logical_shift_left)
                v.tensor_scalar(tq[:], bi[b][:], 0x70, 18, ALU.bitwise_and, ALU.logical_shift_left)
                v.tensor_tensor(ty[:], tp[:], tq[:], ALU.bitwise_or)
                v.tensor_scalar(tp2[:], bi[b][:], 8, 12, ALU.bitwise_and, ALU.logical_shift_left)
                v.tensor_scalar(tq2[:], bi[b][:], 0x80, 24, ALU.bitwise_and, ALU.logical_shift_left)
                v.tensor_tensor(tsb[:], tp2[:], tq2[:], ALU.bitwise_or)
                wd = dec.inc(v.tensor_tensor(yy[:], ty[:], tsb[:], ALU.bitwise_or), 1)
                ybf = yy[:].bitcast(BF16)
                v.wait_ge(rmm.h, 4 * (blk + 1))
                for sub in range(F // 512):
                    lo = ybf[:, 2 * 512 * sub: 2 * 512 * (sub + 1): 2]
                    hi = ybf[:, 2 * 512 * sub + 1: 2 * 512 * (sub + 1): 2]
                    v.tensor_tensor(pl[b][0][:, 512 * sub:512 * (sub + 1)], lo, pss[sub][:], ALU.mult)
                    ptt.inc(v.tensor_tensor(pl[b][1][:, 512 * sub:512 * (sub + 1)], hi, pss[sub][:], ALU.mult), 2)
                s.wait_ge(ptt.h, ptt.n)
                for j in range(F // P):
                    cc0, b0 = hc_of(k, 0, c0 // P + j)
                    cc1, b1 = hc_of(k, 1, c0 // P + j)
                    sos.inc(s.dma_start(out=dest[cc0, b0], in_=pl[b][0][:, P * j:P * (j + 1)]), 16)
                    sos.inc(s.dma_start(out=dest[cc1, b1], in_=pl[b][1][:, P * j:P * (j + 1)]), 16)
                blk += 1
        return sos

    def ph4_dequant(self, scope):
        nc = self.nc
        engs = (nc.sync, nc.gpsimd, nc.vector, nc.scalar, nc.tensor)

        def barrier(sems):
            for eng in engs:
                for sem_ in sems:
                    eng.wait_ge(sem_.h, sem_.n)

        barrier([self.ph3_done, self.ph3_q_done, self.ph3_k_done])
        with ExitStack() as m1:
            d1 = self._dequant_mat(m1, "dqg", self.gWT, self.gST, 8, MLP, self.d_Wg,
                                   lambda k, plane, cc: (cc, k + 8 * plane))
        barrier([d1])
        with ExitStack() as m2:
            d2 = self._dequant_mat(m2, "dqu", self.uWT, self.uST, 8, MLP, self.d_Wu,
                                   lambda k, plane, cc: (cc, k + 8 * plane))
        barrier([d2])
        with ExitStack() as m3:
            d3 = self._dequant_mat(m3, "dqd", self.dWT, self.dST, 64, C, self.d_Wd,
                                   lambda k, plane, hh: (k + 64 * plane, hh))
        self.ph4_done = d3
        self.ph4_dones = [d1, d2, d3]

    def ph5_attention(self, scope):
        nc = self.nc
        s, g, v, sc, t = nc.sync, nc.gpsimd, nc.vector, nc.scalar, nc.tensor
        for eng in (s, g, v, sc, t):
            for sem_ in self.ph4_dones + [self.ph3_done, self.ph3_k_done, self.ph3_q_done]:
                eng.wait_ge(sem_.h, sem_.n)
        Ks = [self.sb(f"p5k{i}", [P, T], BF16, scope) for i in range(2)]
        Vs = [self.sb(f"p5v{i}", [P, T], BF16, scope) for i in range(2)]
        Qs = [self.sb(f"p5q{i}", [P, TOK], BF16, scope) for i in range(2)]
        Eb = [self.sb(f"p5e{i}", [P, TOK], BF16, scope) for i in range(2)]
        sums_sb = self.sb("p5sum", [1, TOK], F32, scope)
        recip = self.sb("p5rcp", [1, TOK], F32, scope)
        bc_sb = self.sb("p5bc", [P, TOK], F32, scope)
        Oo = [self.sb(f"p5o{i}", [P, TOK], BF16, scope) for i in range(2)]
        ps_S = [self.psum(f"p5psS{i}", [P, TOK], F32, scope) for i in range(2)]
        ps_O = self.psum("p5psO", [P, TOK], F32, scope)
        ps_m = [self.psum(f"p5pm{i}", [P, 512], F32, scope) for i in range(2)]
        ps_s0 = ps_m[0][0:1, :]
        ps_s1 = ps_m[1][0:1, :]
        ld = self.sem("p5ld")
        smm = self.sem("p5smm")
        sexp = self.sem("p5sexp")
        svm = self.sem("p5svm")
        spv = self.sem("p5spv")
        ssum = self.sem("p5ssum")
        snrm = self.sem("p5snrm")
        sso = self.sem("p5sso")
        ssb = self.sem("p5ssb")
        nkt_tot = 0
        for h in range(NH):
            hb = h % 2
            if h >= 2:
                s.wait_ge(snrm.h, h - 1)
            wl = ld.inc(s.dma_start(out=Ks[hb][:], in_=self.d_K[P * h:P * (h + 1), :]), 16)
            for ktl in range(T // P):
                wl = ld.inc(s.dma_start(
                    out=Vs[hb][:, P * ktl:P * (ktl + 1)],
                    in_=self.d_V[P * ktl:P * (ktl + 1), P * h:P * (h + 1)]), 16)
            wl = ld.inc(s.dma_start(out=Qs[hb][:], in_=self.d_Q[P * h:P * (h + 1), :]), 16)
            t.wait_ge(ld.h, wl)
            if h >= 1:
                t.wait_ge(snrm.h, h)
            base_exp = sexp.n
            base_pv = spv.n

            def emit_pv(kt, wv_):
                t.wait_ge(svm.h, wv_)
                smin_ = kt // 8
                q0_ = P * smin_
                if smin_ < 4:
                    segs = [(q0_, 512, ps_s0, 0, 39), (512, TOK, ps_s1, 512, 63)]
                else:
                    segs = [(q0_, TOK, ps_s1, 512, 63)]
                for (a, bnd, pss_, sb0, lastkt) in segs:
                    mms = t.matmul(pss_[:, a - sb0:bnd - sb0], self.ones_bf[:],
                                   Eb[kt % 2][:, a:bnd],
                                   start=(kt == 0), stop=(kt == lastkt))
                    if kt == 63:
                        ssb.inc(mms, 1)
                    mm = t.matmul(ps_O[:, a:bnd], Vs[hb][:, P * kt:P * (kt + 1)],
                                  Eb[kt % 2][:, a:bnd],
                                  start=(kt == 0), stop=(kt == lastkt))
                spv.inc(mm, 1)

            pend = None
            for kt in range(T // P):
                smin = kt // 8
                q0 = P * smin
                o = kt % 8
                Sb = ps_S[kt % 2]
                pieces = ([(q0, 512), (512, TOK)] if smin < 4 else [(q0, TOK)])
                if kt >= 2:
                    t.wait_ge(sexp.h, base_exp + kt - 1)
                for pi_, (a, bnd) in enumerate(pieces):
                    wm = smm.inc(t.matmul(Sb[:, a:bnd],
                                          Ks[hb][:, P * kt:P * (kt + 1)],
                                          Qs[hb][:, a:bnd],
                                          start=True, stop=True), 1)
                if pend is not None:
                    emit_pv(*pend)
                sc.wait_ge(smm.h, wm)
                if kt >= 2:
                    sc.wait_ge(spv.h, base_pv + kt - 1)
                we = sexp.inc(sc.activation(Eb[kt % 2][:, q0:TOK], Sb[:, q0:TOK],
                                            AF.Exp, scale=SCALE, bias=EXP_SHIFT), 1)
                v.wait_ge(sexp.h, we)
                wv_ = svm.inc(v.tensor_tensor(Eb[kt % 2][:, q0:q0 + P],
                                              Eb[kt % 2][:, q0:q0 + P],
                                              self.masks_s[:, P * o:P * (o + 1)],
                                              ALU.mult), 1)
                pend = (kt, wv_)
            emit_pv(*pend)
            v.wait_ge(spv.h, spv.n)
            v.wait_ge(ssb.h, ssb.n)
            v.wait_ge(smm.h, smm.n)
            v.tensor_copy(sums_sb[:, 0:512], ps_s0[:])
            v.tensor_copy(sums_sb[:, 512:TOK], ps_s1[:])
            wv_ = ssum.inc(v.reciprocal(recip[:], sums_sb[:]), 1)
            t.wait_ge(ssum.h, wv_)
            wbc0 = smm.inc(t.matmul(ps_m[0][:], self.ones_r32[:], recip[:, 0:512],
                                    start=True, stop=True), 1)
            wbc = smm.inc(t.matmul(ps_m[1][:], self.ones_r32[:], recip[:, 512:TOK],
                                   start=True, stop=True), 1)
            v.wait_ge(smm.h, wbc0)
            wv_ = ssum.inc(v.tensor_copy(bc_sb[:, 0:512], ps_m[0][:]), 1)
            v.wait_ge(smm.h, wbc)
            ssum.inc(v.tensor_copy(bc_sb[:, 512:TOK], ps_m[1][:]), 1)
            v.tensor_tensor(Oo[hb][:, 0:512], ps_O[:, 0:512], bc_sb[:, 0:512], ALU.mult)
            wn = snrm.inc(v.tensor_tensor(Oo[hb][:, 512:TOK], ps_O[:, 512:TOK],
                                          bc_sb[:, 512:TOK], ALU.mult), 1)
            s.wait_ge(snrm.h, wn)
            sso.inc(s.dma_start(out=self.d_O[P * h:P * (h + 1), :], in_=Oo[hb][:]), 16)
        self.ph5_done = sso

    def ph6_proj_ln2(self, scope):
        nc = self.nc
        s, g, v, sc, t = nc.sync, nc.gpsimd, nc.vector, nc.scalar, nc.tensor
        for eng in (s, g, v, sc, t):
            eng.wait_ge(self.ph5_done.h, self.ph5_done.n)
        x1 = self.sb("p6x1", [P, HC * TOK], F32, scope)
        with ExitStack() as pa:
            wp = self.sb("p6wp", [P, HC * C], BF16, pa)
            Os = self.sb("p6o", [P, HC * TOK], BF16, pa)
            xc = [self.sb(f"p6xc{i}", [P, 512], F32, pa) for i in range(2)]
            pps = [self.psum(f"p6ps{i}", [P, 512], F32, pa) for i in range(2)]
            ldw = self.sem("p6ldw")
            ldx = self.sem("p6ldx")
            mme = self.sem("p6mm")
            evs = self.sem("p6ev")
            sxo = self.sem("p6so")
            for hc in range(HC):
                ldw.inc(g.dma_start(out=wp[:, hc * C:(hc + 1) * C],
                                    in_=self.w_proj_p[P * hc:P * (hc + 1), :]), 16)
            for chl in range(HC):
                ldw.inc(s.dma_start(
                    out=Os[:, TOK * chl:TOK * (chl + 1)],
                    in_=self.d_O[P * chl:P * (chl + 1), :]), 16)
            t.wait_ge(ldw.h, ldw.n)
            grp = 0
            for oc in range(HC):
                for half in range(2):
                    pb = grp % 2
                    wlx = ldx.inc(s.dma_start(
                        out=xc[pb][:],
                        in_=self.d_xfm[P * oc:P * (oc + 1),
                                       512 * half:512 * (half + 1)]), 16)
                    if grp >= 2:
                        t.wait_ge(evs.h, grp - 1)
                    for hc in range(HC):
                        mm = t.matmul(pps[pb][:],
                                      wp[:, hc * C + P * oc: hc * C + P * (oc + 1)],
                                      Os[:, hc * TOK + 512 * half: hc * TOK + 512 * (half + 1)],
                                      start=(hc == 0), stop=(hc == HC - 1))
                    wm = mme.inc(mm, 1)
                    v.wait_ge(mme.h, wm)
                    v.wait_ge(ldx.h, wlx)
                    col = oc * TOK + 512 * half
                    we = evs.inc(v.tensor_tensor(x1[:, col:col + 512], pps[pb][:],
                                                 xc[pb][:], ALU.add), 1)
                    s.wait_ge(evs.h, we)
                    sxo.inc(s.dma_start(out=self.d_x1[P * oc:P * (oc + 1),
                                                      512 * half:512 * (half + 1)],
                                        in_=x1[:, col:col + 512]), 16)
                    grp += 1
            self.ph6_x1_done = sxo
        for eng in (s, g, v, sc, t):
            eng.wait_ge(self.ph6_x1_done.h, self.ph6_x1_done.n)
        with ExitStack() as pb_:
            sq6 = self.sb("p6sq", [P, 512], F32, pb_)
            strow_t = self.sb("p6st", [1, 8 * 512], F32, pb_)
            def strow(i):
                return strow_t[0:1, 512 * i:512 * (i + 1)]
            rstd_b = self.sb("p6rb", [P, 512], F32, pb_)
            nmr_b = self.sb("p6nb", [P, 512], F32, pb_)
            tmp6 = self.sb("p6tmp", [P, 512], F32, pb_)
            l2c = self.sb("p6l2", [P, 512], BF16, pb_)
            ps_m1 = self.psum("p6pm1", [1, 512], F32, pb_)
            ps_m2 = self.psum("p6pm2", [1, 512], F32, pb_)
            ps_bc = self.psum("p6pbc", [P, 512], F32, pb_)
            sact = self.sem("p6act")
            scm = self.sem("p6cm")
            spl = self.sem("p6pl")
            smb = self.sem("p6mb")
            sm1 = self.sem("p6m1")
            sso2 = self.sem("p6so2")
            for half in range(2):
                if half == 1:
                    t.wait_ge(scm.h, scm.n)
                for oc in range(HC):
                    col = oc * TOK + 512 * half
                    if half > 0 or oc > 0:
                        sc.wait_ge(smb.h, smb.n)
                    wa = sact.inc(sc.activation(sq6[:], x1[:, col:col + 512], AF.Square), 1)
                    t.wait_ge(sact.h, wa)
                    mm1 = t.matmul(ps_m1[:], self.ones_c32[:], x1[:, col:col + 512],
                                   start=(oc == 0), stop=(oc == HC - 1))
                    if oc == HC - 1:
                        sm1.inc(mm1, 1)
                    wm = smb.inc(t.matmul(ps_m2[:], self.ones_c32[:], sq6[:],
                                          start=(oc == 0), stop=(oc == HC - 1)), 1)
                v.wait_ge(smb.h, smb.n)
                v.wait_ge(sm1.h, sm1.n)
                v.tensor_copy(strow(0), ps_m1[:])
                wv_ = scm.inc(v.tensor_copy(strow(1), ps_m2[:]), 1)
                g.wait_ge(scm.h, wv_)
                g.tensor_scalar(strow(2), strow(0), 1.0 / C, None, ALU.mult)
                g.tensor_scalar(strow(3), strow(1), 1.0 / C, None, ALU.mult)
                g.tensor_tensor(strow(4), strow(2), strow(2), ALU.mult)
                g.tensor_tensor(strow(3), strow(3), strow(4), ALU.subtract)
                wp_ = spl.inc(g.tensor_scalar(strow(3), strow(3), EPS, None, ALU.add), 1)
                sc.wait_ge(spl.h, wp_)
                wa = sact.inc(sc.activation(strow(4), strow(3), AF.Ln), 1)
                sc.wait_ge(sact.h, wa)
                wa = sact.inc(sc.activation(strow(4), strow(4), AF.Exp, scale=-0.5), 1)
                g.wait_ge(sact.h, wa)
                g.tensor_tensor(strow(5), strow(2), strow(4), ALU.mult)
                wp_ = spl.inc(g.tensor_scalar(strow(5), strow(5), -1.0, None, ALU.mult), 1)
                t.wait_ge(spl.h, wp_)
                wm = smb.inc(t.matmul(ps_bc[:], self.ones_r32[:], strow(4),
                                      start=True, stop=True), 1)
                v.wait_ge(smb.h, wm)
                wv_ = scm.inc(v.tensor_copy(rstd_b[:], ps_bc[:]), 1)
                t.wait_ge(scm.h, wv_)
                wm = smb.inc(t.matmul(ps_bc[:], self.ones_r32[:], strow(5),
                                      start=True, stop=True), 1)
                v.wait_ge(smb.h, wm)
                scm.inc(v.tensor_copy(nmr_b[:], ps_bc[:]), 1)
                for oc in range(HC):
                    col = oc * TOK + 512 * half
                    if half > 0 or oc > 0:
                        v.wait_ge(sso2.h, sso2.n)
                    v.tensor_tensor(tmp6[:], x1[:, col:col + 512], rstd_b[:], ALU.mult)
                    wv_ = scm.inc(v.tensor_tensor(l2c[:], tmp6[:], nmr_b[:], ALU.add), 1)
                    s.wait_ge(scm.h, wv_)
                    sso2.inc(s.dma_start(out=self.d_ln2[P * oc:P * (oc + 1),
                                                        512 * half:512 * (half + 1)],
                                         in_=l2c[:]), 16)
            self.ph6_done = sso2

    def ph7_ffn(self, scope):
        nc = self.nc
        s, g, v, sc, t = nc.sync, nc.gpsimd, nc.vector, nc.scalar, nc.tensor
        engs = (s, g, v, sc, t)

        def barrier(sems):
            for eng in engs:
                for sem_ in sems:
                    eng.wait_ge(sem_.h, sem_.n)

        barrier([self.ph6_done, self.ph6_x1_done] + self.ph4_dones)
        yso = self.sem("p7yso")
        for half in range(2):
            if half == 1:
                barrier([yso])
            cols = 512 * half
            with ExitStack() as hs:
                inter = self.sb(f"p7int{half}", [P, NCC * 512], BF16, hs)
                sA = self.sem(f"p7sA{half}")
                with ExitStack() as pa:
                    ln2s = self.sb(f"p7l2{half}", [P, HC * 512], BF16, pa)
                    wgb = [self.sb(f"p7wg{half}_{i}", [P, HC * P], BF16, pa) for i in range(2)]
                    wub = [self.sb(f"p7wu{half}_{i}", [P, HC * P], BF16, pa) for i in range(2)]
                    gel = self.sb(f"p7gel{half}", [P, 512], BF16, pa)
                    ps_g = [self.psum(f"p7pg{half}_{i}", [P, 512], F32, pa) for i in range(2)]
                    ps_u = [self.psum(f"p7pu{half}_{i}", [P, 512], F32, pa) for i in range(2)]
                    ld = self.sem(f"p7ld{half}")
                    smm = self.sem(f"p7mm{half}")
                    sgel = self.sem(f"p7gl{half}")
                    for hcl in range(HC):
                        wl = ld.inc(s.dma_start(
                            out=ln2s[:, 512 * hcl:512 * (hcl + 1)],
                            in_=self.d_ln2[P * hcl:P * (hcl + 1),
                                           cols:cols + 512]), 16)
                    t.wait_ge(ld.h, wl)
                    for cc in range(NCC):
                        b = cc % 2
                        if cc >= 2:
                            g.wait_ge(smm.h, 2 * (cc - 1))
                        wl = ld.inc(g.dma_start(
                            out=wgb[b][:].rearrange("p (h f) -> p h f", f=P),
                            in_=self.d_Wg[cc].rearrange("h p f -> p h f")), 16)
                        wl = ld.inc(g.dma_start(
                            out=wub[b][:].rearrange("p (h f) -> p h f", f=P),
                            in_=self.d_Wu[cc].rearrange("h p f -> p h f")), 16)
                        t.wait_ge(ld.h, wl)
                        if cc >= 2:
                            t.wait_ge(sA.h, cc - 1)
                        for hc in range(HC):
                            gmm = t.matmul(ps_g[b][:], wgb[b][:, P * hc:P * (hc + 1)],
                                           ln2s[:, 512 * hc:512 * (hc + 1)],
                                           start=(hc == 0), stop=(hc == HC - 1))
                        wm = smm.inc(gmm, 1)
                        for hc in range(HC):
                            mm = t.matmul(ps_u[b][:], wub[b][:, P * hc:P * (hc + 1)],
                                          ln2s[:, 512 * hc:512 * (hc + 1)],
                                          start=(hc == 0), stop=(hc == HC - 1))
                        wm = smm.inc(mm, 1)
                        sc.wait_ge(smm.h, wm)
                        if cc >= 1:
                            sc.wait_ge(sA.h, cc)
                        wg_ = sgel.inc(sc.activation(gel[:], ps_g[b][:],
                                                     AF.Gelu_apprx_tanh), 1)
                        v.wait_ge(sgel.h, wg_)
                        sA.inc(v.tensor_tensor(inter[:, 512 * cc:512 * (cc + 1)],
                                               gel[:], ps_u[b][:], ALU.mult), 1)
                barrier([sA])
                if self.phase == 8 and half == 1:
                    dbs = self.sem("p7dbg1")
                    dbs.inc(s.dma_start(out=self.d_dbE[:, :],
                                        in_=inter[:, 56 * 512:56 * 512 + C]), 16)
                    dbs.inc(s.dma_start(out=self.d_dbi[:, :], in_=inter[:, :]), 16)
                    self.dbg1 = dbs
                out_s = self.sb(f"p7out{half}", [P, HC * 512], F32, hs)
                with ExitStack() as pb_:
                    x1c = [self.sb(f"p7x1{half}_{i}", [P, 512], F32, pb_) for i in range(2)]
                    wdb = [self.sb(f"p7wd{half}_{i}", [P, 8 * P], BF16, pb_) for i in range(2)]
                    ps_d = [self.psum(f"p7pd{half}_{i}", [P, 512], F32, pb_) for i in range(8)]
                    ldd = self.sem(f"p7ldd{half}")
                    smd = self.sem(f"p7md{half}")
                    sev = self.sem(f"p7ev{half}")
                    ldx = self.sem(f"p7lx{half}")
                    NIP = NCC + 7
                    for pass_ in range(2):
                        if pass_ == 1:
                            t.wait_ge(sev.h, 8)
                        for cc in range(NCC):
                            b = cc % 2
                            gi = pass_ * NCC + cc
                            if gi >= 2:
                                pp, cp = (gi - 2) // NCC, (gi - 2) % NCC
                                req = NIP * pp + (cp + 1 if cp < NCC - 1 else NIP)
                                g.wait_ge(smd.h, req)
                            for jl in range(8):
                                wl = ldd.inc(g.dma_start(
                                    out=wdb[b][:, P * jl:P * (jl + 1)],
                                    in_=self.d_Wd[cc, 8 * pass_ + jl]), 16)
                            t.wait_ge(ldd.h, wl)
                            for j in range(8):
                                mm = t.matmul(ps_d[j][:], wdb[b][:, P * j:P * (j + 1)],
                                              inter[:, 512 * cc:512 * (cc + 1)],
                                              start=(cc == 0), stop=(cc == NCC - 1))
                                if cc == NCC - 1:
                                    smd.inc(mm, 1)
                            if cc < NCC - 1:
                                smd.inc(mm, 1)
                        for j in range(8):
                            hh = 8 * pass_ + j
                            xb = j % 2
                            if 8 * pass_ + j >= 2:
                                s.wait_ge(sev.h, 8 * pass_ + j - 1)
                            wlx = ldx.inc(s.dma_start(
                                out=x1c[xb][:],
                                in_=self.d_x1[P * hh:P * (hh + 1), cols:cols + 512]), 16)
                            v.wait_ge(smd.h, NIP * (pass_ + 1))
                            v.wait_ge(ldx.h, wlx)
                            sev.inc(v.tensor_tensor(out_s[:, 512 * hh:512 * (hh + 1)],
                                                    ps_d[j][:], x1c[xb][:], ALU.add), 1)
                if self.phase == 8 and half == 1:
                    self.dbg1.inc(s.dma_start(out=self.d_dbg[P:2 * P, :], in_=out_s[:, 0:C]), 16)
                    self.dbg1.inc(s.dma_start(out=self.d_dbL[:, :],
                                              in_=inter[:, 56 * 512:56 * 512 + C]), 16)
                if self.phase >= 9:
                    if half == 0:
                        self.dbg9 = self.sem("p7dbg9")
                    for eng in (s,):
                        eng.wait_ge(sev.h, sev.n)
                    self.dbg9.inc(s.dma_start(out=self.d_dbo[P * half:P * (half + 1), :],
                                              in_=out_s[:, :]), 16)
                barrier([sev])
                if self.phase >= 9 and half == 1:
                    self.dbg9.inc(s.dma_start(out=self.d_dbi[:, :], in_=inter[:, :]), 16)
                with ExitStack() as pc_:
                    asm = self.sb(f"p7asm{half}", [P, C], F32, pc_)
                    ps_t = self.psum(f"p7pt{half}", [P, P], F32, pc_)
                    stp = self.sem(f"p7tp{half}")
                    scp = self.sem(f"p7cp{half}")
                    for t4 in range(4):
                        if t4 > 0:
                            t.wait_ge(yso.h, yso.n)
                        for hh in range(HC):
                            if hh > 0 or t4 > 0:
                                t.wait_ge(scp.h, scp.n)
                            wt = stp.inc(t.transpose(
                                ps_t[:], out_s[:, 512 * hh + P * t4: 512 * hh + P * (t4 + 1)],
                                self.ident3_s[:]), 1)
                            v.wait_ge(stp.h, wt)
                            if hh < 8:
                                dstc = asm[:, 256 * hh: 256 * (hh + 1): 2]
                            else:
                                dstc = asm[:, 256 * (hh - 8) + 1: 256 * (hh - 7): 2]
                            scp.inc(v.tensor_copy(dstc, ps_t[:]), 1)
                        s.wait_ge(scp.h, scp.n)
                        yso.inc(s.dma_start(
                            out=self.y[512 * half + P * t4: 512 * half + P * (t4 + 1), :],
                            in_=asm[:]), 16)
                        if self.phase == 8 and half == 1 and t4 == 0:
                            self.dbg1.inc(s.dma_start(out=self.d_dbg[3 * P:4 * P, :],
                                                      in_=asm[:]), 16)
        self.ph7_done = yso

    def epilogue(self, scope):
        nc = self.nc
        s, v = nc.sync, nc.vector
        fin = self.sem("fin")
        cv = self.sem("dbgcv")
        ld = self.sem("dbgld")

        def dump_dram_bf16(row, src_ap, w=C):
            dbge = self.sb(f"dbge{row}", [P, w], BF16, scope)
            dbgf = self.sb(f"dbgf{row}", [P, w], F32, scope)
            wl = ld.inc(s.dma_start(out=dbge[:], in_=src_ap), 16)
            v.wait_ge(ld.h, wl)
            wc = cv.inc(v.tensor_copy(dbgf[:], dbge[:]), 1)
            s.wait_ge(cv.h, wc)
            fin.inc(s.dma_start(out=self.y[row * P:(row + 1) * P, 0:w], in_=dbgf[:]), 16)

        def dump_dram_f32(row, src_ap, w=C):
            dbgf = self.sb(f"dbgg{row}", [P, w], F32, scope)
            wl = ld.inc(s.dma_start(out=dbgf[:], in_=src_ap), 16)
            s.wait_ge(ld.h, wl)
            fin.inc(s.dma_start(out=self.y[row * P:(row + 1) * P, 0:w], in_=dbgf[:]), 16)

        if self.phase >= 7:
            s.wait_ge(self.ph7_done.h, self.ph7_done.n)
            if self.phase == 8:
                s.wait_ge(self.dbg1.h, self.dbg1.n)
                dump_dram_bf16(0, self.d_dbE[:, :])
                dump_dram_f32(1, self.d_dbg[P:2 * P, :])
                dump_dram_bf16(2, self.d_dbL[:, :])
                dump_dram_f32(3, self.d_dbg[3 * P:4 * P, :])
        elif self.phase >= 6:
            s.wait_ge(self.ph6_done.h, self.ph6_done.n)
            s.wait_ge(self.ph6_x1_done.h, self.ph6_x1_done.n)
            dump_dram_f32(0, self.d_x1[0:P, :], TOK)
            dump_dram_bf16(1, self.d_ln2[0:P, :], TOK)
            dump_dram_f32(2, self.d_x1[5 * P:6 * P, :], TOK)
        elif self.phase >= 5:
            s.wait_ge(self.ph5_done.h, self.ph5_done.n)
            dump_dram_bf16(0, self.d_O[0:P, :], TOK)
            dump_dram_bf16(1, self.d_O[P:2 * P, :], TOK)
        elif self.phase >= 4:
            for sem_ in self.ph4_dones:
                s.wait_ge(sem_.h, sem_.n)

            def dump_blocks(row, dest, blk):
                dbge = self.sb(f"dbge{row}", [P, C], BF16, scope)
                dbgf = self.sb(f"dbgf{row}", [P, C], F32, scope)
                wl = ld.n
                for cc in range(16):
                    wl = ld.inc(s.dma_start(out=dbge[:, P * cc:P * (cc + 1)],
                                            in_=dest[cc, blk]), 16)
                v.wait_ge(ld.h, wl)
                wc = cv.inc(v.tensor_copy(dbgf[:], dbge[:]), 1)
                s.wait_ge(cv.h, wc)
                fin.inc(s.dma_start(out=self.y[row * P:(row + 1) * P, :], in_=dbgf[:]), 16)

            dump_blocks(0, self.d_Wg, 3)
            dump_blocks(1, self.d_Wu, 11)
            dump_blocks(2, self.d_Wd, 5)
        elif self.phase >= 3:
            for sem_ in (self.ph3_done, self.ph3_k_done, self.ph3_q_done):
                s.wait_ge(sem_.h, sem_.n)
            dump_dram_bf16(0, self.d_K[0:P, 0:C])
            dump_dram_bf16(1, self.d_V[0:P, 0:C])
            dump_dram_bf16(2, self.d_Q[0:P, :], TOK)
        elif self.phase >= 2:
            s.wait_ge(self.ph2_done.h, self.ph2_done.n)
            dump_dram_bf16(0, self.d_ln1fmT[0:P, 0:C])
        else:
            s.wait_ge(self.ph1_full_done.h, self.ph1_full_done.n)
            s.wait_ge(self.ph1_own_done.h, self.ph1_own_done.n)
            dump_dram_bf16(0, self.d_ln1tm[0:P, :])
            dump_dram_bf16(1, self.d_lo[0:P, :], TOK)
            dump_dram_f32(2, self.d_xfm[0:P, :], TOK)
        s.wait_ge(fin.h, fin.n)


def kernel(**inputs):
    b = Builder(phase=7)
    nc = b.build()
    in_maps = [host_prep(inputs, core) for core in range(8)]
    from concourse.bass_utils import run_bass_kernel_spmd
    res = run_bass_kernel_spmd(nc, in_maps, core_ids=list(range(8)))
    out = np.empty((T, C), np.float32)
    for core in range(8):
        out[own_rows(core)] = np.asarray(res.results[core]["y"], np.float32)
    return out.reshape(1, T, C)


# version 04 (final)
# speedup vs baseline: 1.4048x; 1.0640x over previous
import numpy as np
import ml_dtypes
import concourse.bass as bass
import concourse.mybir as mybir
from contextlib import ExitStack

dt = mybir.dt
AF = mybir.ActivationFunctionType
ALU = mybir.AluOpType
AX = mybir.AxisListType.X
F32, BF16, I32, U8 = dt.float32, dt.bfloat16, dt.int32, dt.uint8
BF = ml_dtypes.bfloat16

P = 128
T = 8192
C = 2048
NH = 16
D = 128
NS = 8
TOK = NS * P
MLP = 16384
NCC = MLP // P
HC = C // P
SCALE = float(1.0 / np.sqrt(128.0))
EXP_SHIFT = -12.0
EPS = 1e-5

PI = np.concatenate([np.arange(0, C, 2), np.arange(1, C, 2)])
PIC = np.concatenate([np.arange(0, MLP, 2), np.arange(1, MLP, 2)])


def own_rows(core):
    return np.concatenate(
        [np.arange(P * (8 * s + core), P * (8 * s + core) + P) for s in range(NS)])


def host_prep(inputs, core):
    x = np.asarray(inputs["L0_attn_x"])[0]
    w_attn = np.asarray(inputs["L0_attn_w_attn"])
    w_proj = np.asarray(inputs["L0_attn_w_proj"])
    gW = np.asarray(inputs["L0_ffn_gate_W_packed"])
    gS = np.asarray(inputs["L0_ffn_gate_scales"])
    uW = np.asarray(inputs["L0_ffn_up_W_packed"])
    uS = np.asarray(inputs["L0_ffn_up_scales"])
    dW = np.asarray(inputs["L0_ffn_down_W_packed"])
    dS = np.asarray(inputs["L0_ffn_down_scales"])
    rows = own_rows(core)
    m = {}
    m["x_full"] = np.ascontiguousarray(x, dtype=np.float32)
    m["x_own"] = np.ascontiguousarray(x[rows], dtype=np.float32)
    m["w_attn_p"] = np.ascontiguousarray(w_attn[PI, :], dtype=np.float32)
    m["w_proj_p"] = np.ascontiguousarray(w_proj[:, PI], dtype=np.float32)
    m["gWT"] = np.ascontiguousarray(gW[PIC].astype(np.uint8).T)
    m["uWT"] = np.ascontiguousarray(uW[PIC].astype(np.uint8).T)
    m["dWT"] = np.ascontiguousarray(dW[PI].astype(np.uint8).T)
    m["gST"] = np.ascontiguousarray(np.repeat(gS[PIC].T, 16, axis=0), dtype=np.float32)
    m["uST"] = np.ascontiguousarray(np.repeat(uS[PIC].T, 16, axis=0), dtype=np.float32)
    m["dST"] = np.ascontiguousarray(np.repeat(dS[PI].T, 16, axis=0), dtype=np.float32)
    m["ident32"] = np.eye(P, dtype=np.float32)
    m["identbf"] = np.eye(P, dtype=BF)
    kk = np.arange(P)[None, :, None]
    qq = np.arange(P)[None, None, :]
    oo = np.arange(NS)[:, None, None]
    m["masks"] = (128 * oo + kk <= 128 * core + qq).astype(BF)
    R = np.zeros((8, P), np.float32)
    for g_ in range(8):
        R[g_, 16 * g_:16 * g_ + 16] = float(2.0 ** 126)
    m["repl"] = R
    return m


class Sem:
    __slots__ = ("h", "n")

    def __init__(self, h):
        self.h = h
        self.n = 0

    def inc(self, instr, amt=1):
        instr.then_inc(self.h, amt)
        self.n += amt
        return self.n


class Builder:
    def __init__(self, phase=99):
        self.phase = phase
        self.nc = bass.Bass("TRN2", target_bir_lowering=False)
        self.es = ExitStack()
        self.es.__enter__()
        self._semid = 0

    def sb(self, name, shape, dtype, scope=None):
        return (scope or self.es).enter_context(self.nc.sbuf_tensor(name, shape, dtype))

    def psum(self, name, shape, dtype=F32, scope=None):
        return (scope or self.es).enter_context(self.nc.psum_tensor(name, shape, dtype))

    def sem(self, name=None):
        self._semid += 1
        h = self.es.enter_context(self.nc.semaphore(name or f"s{self._semid}"))
        return Sem(h)

    def dram(self, name, shape, dtype, kind="Internal"):
        return self.nc.dram_tensor(name, shape, dtype, kind=kind).ap()

    def build(self):
        nc = self.nc
        self.x_full = self.dram("x_full", [T, C], F32, "ExternalInput")
        self.x_own = self.dram("x_own", [TOK, C], F32, "ExternalInput")
        self.w_attn_p = self.dram("w_attn_p", [C, 3 * C], F32, "ExternalInput")
        self.w_proj_p = self.dram("w_proj_p", [C, C], F32, "ExternalInput")
        self.gWT = self.dram("gWT", [C // 2, MLP], U8, "ExternalInput")
        self.uWT = self.dram("uWT", [C // 2, MLP], U8, "ExternalInput")
        self.dWT = self.dram("dWT", [MLP // 2, C], U8, "ExternalInput")
        self.gST = self.dram("gST", [16 * 64, MLP], F32, "ExternalInput")
        self.uST = self.dram("uST", [16 * 64, MLP], F32, "ExternalInput")
        self.dST = self.dram("dST", [16 * 512, C], F32, "ExternalInput")
        self.ident32 = self.dram("ident32", [P, P], F32, "ExternalInput")
        self.identbf = self.dram("identbf", [P, P], BF16, "ExternalInput")
        self.masks = self.dram("masks", [NS, P, P], BF16, "ExternalInput")
        self.repl = self.dram("repl", [8, P], F32, "ExternalInput")

        self.d_ln1tm = self.dram("d_ln1tm", [T, C], BF16)
        self.d_ln1fmT = self.dram("d_ln1fmT", [C, T], BF16)
        self.d_lo = self.dram("d_lo", [C, TOK], BF16)
        self.d_xfm = self.dram("d_xfm", [C, TOK], F32)
        self.d_K = self.dram("d_K", [C, T], BF16)
        self.d_V = self.dram("d_V", [T, C], BF16)
        self.d_Q = self.dram("d_Q", [C, TOK], BF16)
        self.d_O = self.dram("d_O", [C, TOK], BF16)
        self.d_x1 = self.dram("d_x1", [C, TOK], F32)
        self.d_ln2 = self.dram("d_ln2", [C, TOK], BF16)
        self.d_Wg = self.dram("d_Wg", [NCC, HC, P, P], BF16)
        self.d_Wu = self.dram("d_Wu", [NCC, HC, P, P], BF16)
        self.d_Wd = self.dram("d_Wd", [NCC, HC, P, P], BF16)
        self.d_dbg = self.dram("d_dbg", [4 * P, C], F32)
        self.d_dbE = self.dram("d_dbE", [P, C], BF16)
        self.d_dbi = self.dram("d_dbi", [P, NCC * 512], BF16)
        self.d_dbx = self.dram("d_dbx", [4 * P, TOK], F32)
        self.d_dbo = self.dram("d_dbo", [2 * P, HC * 512], F32)
        self.d_dbL = self.dram("d_dbL", [P, C], BF16)

        self.y = self.dram("y", [TOK, C], F32, "ExternalOutput")

        self.identb_s = self.sb("identb_s", [P, P], BF16)
        self.ident3_s = self.sb("ident3_s", [P, P], F32)
        self.ones_bf = self.sb("ones_bf", [P, 1], BF16)
        self.masks_s = self.sb("masks_s", [P, NS * P], BF16)
        self.repl_s = self.sb("repl_s", [8, P], F32)
        self.consts = self.sb("consts", [P, 2], F32)

        s, v = nc.sync, nc.vector
        ld0 = self.sem("ld0")
        ld0.inc(s.dma_start(out=self.identb_s[:], in_=self.identbf[:]), 16)
        ld0.inc(s.dma_start(out=self.ident3_s[:], in_=self.ident32[:]), 16)
        ld0.inc(s.dma_start(out=self.masks_s[:].rearrange("k (o q) -> k o q", q=P),
                            in_=self.masks[:].rearrange("o k q -> k o q")), 16)
        ld0.inc(s.dma_start(out=self.repl_s[:], in_=self.repl[:]), 16)
        self.ones_r32 = self.sb("ones_r32", [1, P], F32)
        self.ones_c32 = self.sb("ones_c32", [P, 1], F32)
        v.wait_ge(ld0.h, ld0.n)
        v.memset(self.ones_bf[:], 1.0)
        v.memset(self.ones_r32[:], 1.0)
        v.memset(self.ones_c32[:], 1.0)
        self.c_rdy = self.sem("c_rdy")
        v.memset(self.consts[:, 0:1], EXP_SHIFT)
        nc.const_aps.register(F32, EXP_SHIFT, self.consts[:, 0:1])
        self.c_rdy.inc(v.memset(self.consts[:, 1:2], SCALE), 1)
        nc.const_aps.register(F32, SCALE, self.consts[:, 1:2])
        self.ld0 = ld0

        with ExitStack() as sc1:
            self.ph1_ln1(sc1)
        if self.phase >= 2:
            with ExitStack() as sc2:
                self.ph2_bounce(sc2)
        if self.phase >= 3:
            with ExitStack() as sc3:
                self.ph3_kvq(sc3)
        if self.phase == 4:
            with ExitStack() as sc4:
                self.ph4_dequant(sc4)
        if self.phase >= 5:
            with ExitStack() as sc4b, ExitStack() as sc5:
                self.dq_gen = self._dequant_setup(sc4b)
                self.ph5_attention(sc5)
                for _ in self.dq_gen:
                    pass
                self.ph4_dones = [self.dq_sos]
        if self.phase >= 6:
            with ExitStack() as sc6:
                self.ph6_proj_ln2(sc6)
        if self.phase >= 7:
            with ExitStack() as sc7:
                self.ph7_ffn(sc7)
        with ExitStack() as sce:
            self.epilogue(sce)
        self.es.__exit__(None, None, None)
        return nc

    def _ln_tile(self, xin, lnout, st, sq, s_act, s_cm):
        nc = self.nc
        v, sc, pool = nc.vector, nc.scalar, nc.gpsimd
        wv = s_cm.inc(v.reduce_sum(st[:, 0:1], xin, axis=AX), 1)
        wa = s_act.inc(sc.activation(sq[:], xin, AF.Square, accum_out=st[:, 1:2]), 1)
        pool.wait_ge(s_cm.h, wv)
        pool.wait_ge(s_act.h, wa)
        pool.tensor_scalar(st[:, 2:3], st[:, 0:1], 1.0 / C, None, ALU.mult)
        pool.tensor_scalar(st[:, 3:4], st[:, 1:2], 1.0 / C, None, ALU.mult)
        pool.tensor_tensor(st[:, 4:5], st[:, 2:3], st[:, 2:3], ALU.mult)
        pool.tensor_tensor(st[:, 3:4], st[:, 3:4], st[:, 4:5], ALU.subtract)
        wp = s_cm.inc(pool.tensor_scalar(st[:, 3:4], st[:, 3:4], EPS, None, ALU.add), 1)
        sc.wait_ge(s_cm.h, wp)
        wa = s_act.inc(sc.activation(st[:, 5:6], st[:, 3:4], AF.Ln), 1)
        sc.wait_ge(s_act.h, wa)
        wa = s_act.inc(sc.activation(st[:, 5:6], st[:, 5:6], AF.Exp, scale=-0.5), 1)
        pool.wait_ge(s_act.h, wa)
        pool.tensor_tensor(st[:, 6:7], st[:, 2:3], st[:, 5:6], ALU.mult)
        pool.tensor_scalar(st[:, 6:7], st[:, 6:7], -1.0, None, ALU.mult)
        pool.tensor_scalar(lnout[:, 0:C // 2], xin[:, 0:C:2],
                           st[:, 5:6], st[:, 6:7], ALU.mult, ALU.add)
        return pool.tensor_scalar(lnout[:, C // 2:C], xin[:, 1:C:2],
                                  st[:, 5:6], st[:, 6:7], ALU.mult, ALU.add)

    def ph1_ln1(self, scope):
        nc = self.nc
        s, v, sc, g, t = nc.sync, nc.vector, nc.scalar, nc.gpsimd, nc.tensor
        NB = 3
        xt = [self.sb(f"p1x{i}", [P, C], F32, scope) for i in range(NB)]
        lt = [self.sb(f"p1l{i}", [P, C], BF16, scope) for i in range(NB)]
        sq = self.sb("p1sq", [P, C], F32, scope)
        stats = [self.sb(f"p1st{i}", [P, 8], F32, scope) for i in range(NB)]
        ld = self.sem("p1ld")
        act = self.sem("p1act")
        cm = self.sem("p1cm")
        ap = self.sem("p1ap")
        so = self.sem("p1so")
        ntile = T // P
        for i in range(ntile):
            b = i % NB
            if i >= NB:
                s.wait_ge(so.h, 16 * (i - NB + 1))
            wl = ld.inc(s.dma_start(out=xt[b][:], in_=self.x_full[P * i:P * (i + 1), :]), 16)
            v.wait_ge(ld.h, wl)
            sc.wait_ge(ld.h, wl)
            g.wait_ge(ld.h, wl)
            if i >= NB:
                v.wait_ge(ap.h, 2 * (i - NB + 1))
                g.wait_ge(so.h, 16 * (i - NB + 1))
            wp = ap.inc(self._ln_tile(xt[b][:], lt[b][:], stats[b], sq, act, cm), 2)
            s.wait_ge(ap.h, wp)
            so.inc(s.dma_start(out=self.d_ln1tm[P * i:P * (i + 1), :], in_=lt[b][:]), 16)
        self.ph1_full_done = so

        xo = [self.sb(f"p1ox{i}", [P, C], F32, scope) for i in range(2)]
        lo = [self.sb(f"p1ol{i}", [P, C], BF16, scope) for i in range(2)]
        sqo = self.sb("p1osq", [P, C], F32, scope)
        stato = [self.sb(f"p1ost{i}", [P, 8], F32, scope) for i in range(2)]
        evx = self.sb("p1oevx", [P, P], F32, scope)
        evl = self.sb("p1oevl", [P, P], BF16, scope)
        ldo = self.sem("p1old")
        acto = self.sem("p1oact")
        cmo = self.sem("p1ocm")
        apo = self.sem("p1oap")
        mmo = self.sem("p1omm")
        evo = self.sem("p1oev")
        soo = self.sem("p1oso")
        ps32 = self.psum("p1ops32", [P, P], F32, scope)
        psbf = self.psum("p1opsbf", [P, P], BF16, scope)
        for i in range(NS):
            b = i % 2
            if i >= 2:
                g.wait_ge(soo.h, 32 * HC * (i - 1))
            wl = ldo.inc(g.dma_start(out=xo[b][:], in_=self.x_own[P * i:P * (i + 1), :]), 16)
            v.wait_ge(ldo.h, wl)
            sc.wait_ge(ldo.h, wl)
            g.wait_ge(ldo.h, wl)
            if i >= 2:
                v.wait_ge(apo.h, 2 * (i - 1))
            wp = apo.inc(self._ln_tile(xo[b][:], lo[b][:], stato[b], sqo, acto, cmo), 2)
            t.wait_ge(ldo.h, wl)
            t.wait_ge(apo.h, wp)
            for hc in range(HC):
                if hc < 8:
                    cols = xo[b][:, 2 * P * hc: 2 * P * (hc + 1): 2]
                else:
                    cols = xo[b][:, 1 + 2 * P * (hc - 8): 2 * P * (hc - 7): 2]
                if i > 0 or hc > 0:
                    t.wait_ge(evo.h, evo.n)
                mmo.inc(t.transpose(ps32[:], cols, self.ident3_s[:]), 1)
                wm = mmo.inc(t.transpose(psbf[:], lo[b][:, P * hc:P * (hc + 1)],
                                         self.identb_s[:]), 1)
                v.wait_ge(mmo.h, wm)
                if i > 0 or hc > 0:
                    v.wait_ge(soo.h, soo.n)
                evo.inc(v.tensor_copy(evx[:], ps32[:]), 1)
                we = evo.inc(v.tensor_copy(evl[:], psbf[:]), 1)
                s.wait_ge(evo.h, we)
                soo.inc(s.dma_start(out=self.d_xfm[P * hc:P * (hc + 1),
                                                   P * i:P * (i + 1)], in_=evx[:]), 16)
                soo.inc(s.dma_start(out=self.d_lo[P * hc:P * (hc + 1),
                                                  P * i:P * (i + 1)], in_=evl[:]), 16)
        self.ph1_own_done = soo

    def ph2_bounce(self, scope):
        nc = self.nc
        s = nc.sync
        tb = [self.sb(f"p2t{i}", [P, T], BF16, scope) for i in range(2)]
        ld = self.sem("p2ld")
        so = self.sem("p2so")
        s.wait_ge(self.ph1_full_done.h, self.ph1_full_done.n)
        s.wait_ge(self.ph1_own_done.h, self.ph1_own_done.n)
        for k in range(HC):
            b = k % 2
            if k >= 2:
                s.wait_ge(so.h, 16 * (k - 1))
            for jt in range(16):
                wl = ld.inc(s.dma_start_transpose(
                    out=tb[b][:, 512 * jt:512 * (jt + 1)],
                    in_=self.d_ln1tm[512 * jt:512 * (jt + 1), P * k:P * (k + 1)]), 16)
            s.wait_ge(ld.h, wl)
            so.inc(s.dma_start(out=self.d_ln1fmT[P * k:P * (k + 1), :], in_=tb[b][:]), 16)
        self.ph2_done = so

    def _proj_pass(self, scope, tag, w_cols, out_mode, after=()):
        nc = self.nc
        s, g, v, sc, t = nc.sync, nc.gpsimd, nc.vector, nc.scalar, nc.tensor
        for eng in (s, g, v, sc, t):
            for sem_ in after:
                eng.wait_ge(sem_.h, sem_.n)
        W = 2048
        wsb = self.sb(f"{tag}w", [P, HC * W], BF16, scope)
        ldw = self.sem(f"{tag}ldw")
        for hc in range(HC):
            ldw.inc(g.dma_start(out=wsb[:, hc * W:(hc + 1) * W],
                                in_=self.w_attn_p[P * hc:P * (hc + 1),
                                                  w_cols[0]:w_cols[1]]), 16)
        rb = [self.sb(f"{tag}r{i}", [P, HC * 512], BF16, scope) for i in range(2)]
        ev = [self.sb(f"{tag}e{i}", [P, 512], BF16, scope) for i in range(4)]
        pss = [self.psum(f"{tag}p{i}", [P, 512], F32, scope) for i in range(4)]
        ld = self.sem(f"{tag}ld")
        mme = self.sem(f"{tag}mm")
        evs = self.sem(f"{tag}ev")
        sos = self.sem(f"{tag}so")
        t.wait_ge(ldw.h, ldw.n)
        grp = 0
        tr_mme = []
        NTR = T // 512
        for tr in range(NTR):
            b = tr % 2
            if tr >= 2:
                s.wait_ge(mme.h, tr_mme[tr - 2])
            for hcl in range(HC):
                wl = ld.inc(s.dma_start(
                    out=rb[b][:, 512 * hcl:512 * (hcl + 1)],
                    in_=self.d_ln1fmT[P * hcl:P * (hcl + 1),
                                      512 * tr:512 * (tr + 1)]), 16)
            t.wait_ge(ld.h, wl)
            if out_mode == "fm":
                sub = [("fm", kc) for kc in range(HC)]
            else:
                sub = [("tm", (tt, vc)) for tt in range(4) for vc in range(4)]
            for mode, idx in sub:
                pb = grp % 4
                if grp >= 4:
                    t.wait_ge(evs.h, grp - 3)
                if mode == "fm":
                    kc = idx
                    for hc in range(HC):
                        mm = t.matmul(pss[pb][:],
                                      wsb[:, hc * W + P * kc: hc * W + P * (kc + 1)],
                                      rb[b][:, hc * 512:(hc + 1) * 512],
                                      start=(hc == 0), stop=(hc == HC - 1))
                else:
                    tt, vc = idx
                    for hc in range(HC):
                        mm = t.matmul(pss[pb][:],
                                      rb[b][:, hc * 512 + P * tt: hc * 512 + P * (tt + 1)],
                                      wsb[:, hc * W + 512 * vc: hc * W + 512 * (vc + 1)],
                                      start=(hc == 0), stop=(hc == HC - 1))
                wm = mme.inc(mm, 1)
                eng = v if grp % 2 == 0 else sc
                eng.wait_ge(mme.h, wm)
                if grp >= 4:
                    eng.wait_ge(sos.h, 16 * (grp - 3))
                if eng is v:
                    we = evs.inc(v.tensor_copy(ev[pb][:], pss[pb][:]), 1)
                else:
                    we = evs.inc(sc.copy(ev[pb][:], pss[pb][:]), 1)
                s.wait_ge(evs.h, we)
                if mode == "fm":
                    dst = self.d_K[P * idx:P * (idx + 1), 512 * tr:512 * (tr + 1)]
                else:
                    tt, vc = idx
                    dst = self.d_V[512 * tr + P * tt: 512 * tr + P * (tt + 1),
                                   512 * vc:512 * (vc + 1)]
                sos.inc(s.dma_start(out=dst, in_=ev[pb][:]), 16)
                grp += 1
            tr_mme.append(mme.n)
        return sos

    def ph3_kvq(self, scope):
        nc = self.nc
        s, g, v, t = nc.sync, nc.gpsimd, nc.vector, nc.tensor
        with ExitStack() as kp:
            self.ph3_k_done = self._proj_pass(
                kp, "p3k", (C, 2 * C), "fm",
                after=(self.ph1_full_done, self.ph1_own_done, self.ph2_done))
        with ExitStack() as vp:
            self.ph3_done = self._proj_pass(vp, "p3v", (2 * C, 3 * C), "tm",
                                            after=(self.ph3_k_done,))
        with ExitStack() as qp:
            for eng in (s, g, v, t):
                eng.wait_ge(self.ph3_done.h, self.ph3_done.n)
            wq = self.sb("p3qw", [P, HC * C], BF16, qp)
            ldw = self.sem("p3qldw")
            for hc in range(HC):
                ldw.inc(g.dma_start(out=wq[:, hc * C:(hc + 1) * C],
                                    in_=self.w_attn_p[P * hc:P * (hc + 1), 0:C]), 16)
            lob = self.sb("p3qlo", [P, HC * 512], BF16, qp)
            ev = [self.sb(f"p3qe{i}", [P, 512], BF16, qp) for i in range(2)]
            pss = [self.psum(f"p3qp{i}", [P, 512], F32, qp) for i in range(2)]
            ld = self.sem("p3qld")
            mme = self.sem("p3qmm")
            evs = self.sem("p3qev")
            sos = self.sem("p3qso")
            t.wait_ge(ldw.h, ldw.n)
            g.wait_ge(self.ph1_own_done.h, self.ph1_own_done.n)
            grp = 0
            for half in range(2):
                if half == 1:
                    g.wait_ge(mme.h, mme.n)
                for hcl in range(HC):
                    wl = ld.inc(g.dma_start(
                        out=lob[:, 512 * hcl:512 * (hcl + 1)],
                        in_=self.d_lo[P * hcl:P * (hcl + 1),
                                      512 * half:512 * (half + 1)]), 16)
                t.wait_ge(ld.h, wl)
                for h in range(NH):
                    pb = grp % 2
                    if grp >= 2:
                        t.wait_ge(evs.h, grp - 1)
                    for hc in range(HC):
                        mm = t.matmul(pss[pb][:],
                                      wq[:, hc * C + P * h: hc * C + P * (h + 1)],
                                      lob[:, hc * 512:(hc + 1) * 512],
                                      start=(hc == 0), stop=(hc == HC - 1))
                    wm = mme.inc(mm, 1)
                    v.wait_ge(mme.h, wm)
                    if grp >= 2:
                        v.wait_ge(sos.h, 16 * (grp - 1))
                    we = evs.inc(v.tensor_copy(ev[pb][:], pss[pb][:]), 1)
                    s.wait_ge(evs.h, we)
                    sos.inc(s.dma_start(
                        out=self.d_Q[P * h:P * (h + 1), 512 * half:512 * (half + 1)],
                        in_=ev[pb][:]), 16)
                    grp += 1
            self.ph3_q_done = sos

    def _dequant_setup(self, scope):
        nc = self.nc
        s, g, v = nc.sync, nc.gpsimd, nc.vector
        F = 2048
        bi = [self.sb(f"dq_b{i}", [P, F], I32, scope) for i in range(2)]
        tp = self.sb("dq_tp", [P, F], I32, scope)
        tq = self.sb("dq_tq", [P, F], I32, scope)
        ty = self.sb("dq_ty", [P, F], I32, scope)
        tsb = self.sb("dq_tsb", [P, F], I32, scope)
        yy = self.sb("dq_y", [P, F], I32, scope)
        pl = [[self.sb(f"dq_pl{i}_{pn}", [P, F], BF16, scope) for pn in range(2)]
              for i in range(2)]
        ssc = [self.sb(f"dq_ss{i}", [P, F], F32, scope) for i in range(2)]
        ld = self.sem("dq_ld")
        lsc = self.sem("dq_lsc")
        dec = self.sem("dq_dec")
        ptt = self.sem("dq_ptt")
        sos = self.sem("dq_so")
        self.dq_sos = sos
        SC126 = float(2.0 ** 126)

        def gen():
            blk = 0
            mats = [(self.gWT, self.gST, 8, MLP, self.d_Wg,
                     lambda k, plane, cc: (cc, k + 8 * plane)),
                    (self.uWT, self.uST, 8, MLP, self.d_Wu,
                     lambda k, plane, cc: (cc, k + 8 * plane)),
                    (self.dWT, self.dST, 64, C, self.d_Wd,
                     lambda k, plane, hh: (k + 64 * plane, hh))]
            for (src, scalesT, JC, CW, dest, hc_of) in mats:
                nCB = CW // F
                for k in range(JC):
                    for cb in range(nCB):
                        b = blk % 2
                        c0 = cb * F
                        if blk >= 2:
                            g.wait_ge(dec.h, blk - 1)
                            g.wait_ge(ptt.h, 2 * (blk - 1))
                        wl = ld.inc(g.dma_start(out=bi[b][:],
                                                in_=src[P * k:P * (k + 1), c0:c0 + F]), 16)
                        wsc = lsc.inc(g.dma_start(
                            out=ssc[b][:],
                            in_=scalesT[P * k:P * (k + 1), c0:c0 + F]), 16)
                        v.wait_ge(ld.h, wl)
                        if blk >= 2:
                            v.wait_ge(sos.h, 16 * 32 * (blk - 1))
                        v.tensor_scalar(tp[:], bi[b][:], 7, 6,
                                        ALU.bitwise_and, ALU.logical_shift_left)
                        v.tensor_scalar(tq[:], bi[b][:], 0x70, 18,
                                        ALU.bitwise_and, ALU.logical_shift_left)
                        v.tensor_tensor(ty[:], tp[:], tq[:], ALU.bitwise_or)
                        v.tensor_scalar(tp[:], bi[b][:], 8, 12,
                                        ALU.bitwise_and, ALU.logical_shift_left)
                        v.tensor_scalar(tq[:], bi[b][:], 0x80, 24,
                                        ALU.bitwise_and, ALU.logical_shift_left)
                        v.tensor_tensor(tsb[:], tp[:], tq[:], ALU.bitwise_or)
                        dec.inc(v.tensor_tensor(yy[:], ty[:], tsb[:], ALU.bitwise_or), 1)
                        ybf = yy[:].bitcast(BF16)
                        v.wait_ge(lsc.h, wsc)
                        v.scalar_tensor_tensor(pl[b][0][:], ybf[:, 0:2 * F:2],
                                               SC126, ssc[b][:], ALU.mult, ALU.mult)
                        ptt.inc(v.scalar_tensor_tensor(pl[b][1][:], ybf[:, 1:2 * F:2],
                                                       SC126, ssc[b][:],
                                                       ALU.mult, ALU.mult), 2)
                        s.wait_ge(ptt.h, ptt.n)
                        for j in range(F // P):
                            cc0, b0 = hc_of(k, 0, c0 // P + j)
                            cc1, b1 = hc_of(k, 1, c0 // P + j)
                            sos.inc(s.dma_start(out=dest[cc0, b0],
                                                in_=pl[b][0][:, P * j:P * (j + 1)]), 16)
                            sos.inc(s.dma_start(out=dest[cc1, b1],
                                                in_=pl[b][1][:, P * j:P * (j + 1)]), 16)
                        blk += 1
                        yield

        return gen()

    def ph4_dequant(self, scope):
        nc = self.nc
        engs = (nc.sync, nc.gpsimd, nc.vector, nc.scalar, nc.tensor)
        for eng in engs:
            for sem_ in (self.ph3_done, self.ph3_q_done, self.ph3_k_done):
                eng.wait_ge(sem_.h, sem_.n)
        gen = self._dequant_setup(scope)
        for _ in gen:
            pass
        self.ph4_dones = [self.dq_sos]

    def ph5_attention(self, scope):
        nc = self.nc
        s, g, v, sc, t = nc.sync, nc.gpsimd, nc.vector, nc.scalar, nc.tensor
        for eng in (s, g, v, sc, t):
            for sem_ in (self.ph3_done, self.ph3_k_done, self.ph3_q_done):
                eng.wait_ge(sem_.h, sem_.n)
        Ks = [self.sb(f"p5k{i}", [P, T], BF16, scope) for i in range(2)]
        Vs = [self.sb(f"p5v{i}", [P, T], BF16, scope) for i in range(2)]
        Qs = [self.sb(f"p5q{i}", [P, TOK], BF16, scope) for i in range(2)]
        Eb = [self.sb(f"p5e{i}", [P, TOK], BF16, scope) for i in range(2)]
        sums_sb = self.sb("p5sum", [1, TOK], F32, scope)
        recip = self.sb("p5rcp", [1, TOK], F32, scope)
        bc_sb = self.sb("p5bc", [P, TOK], F32, scope)
        Oo = [self.sb(f"p5o{i}", [P, TOK], BF16, scope) for i in range(2)]
        ps_S = [self.psum(f"p5psS{i}", [P, TOK], F32, scope) for i in range(2)]
        ps_O = self.psum("p5psO", [P, TOK], F32, scope)
        ps_m = [self.psum(f"p5pm{i}", [P, 512], F32, scope) for i in range(2)]
        ps_s0 = ps_m[0][0:1, :]
        ps_s1 = ps_m[1][0:1, :]
        ld = self.sem("p5ld")
        smm = self.sem("p5smm")
        sexp = self.sem("p5sexp")
        svm = self.sem("p5svm")
        spv = self.sem("p5spv")
        ssum = self.sem("p5ssum")
        snrm = self.sem("p5snrm")
        sso = self.sem("p5sso")
        ssb = self.sem("p5ssb")
        nkt_tot = 0
        for h in range(NH):
            hb = h % 2
            if h >= 2:
                s.wait_ge(snrm.h, h - 1)
            wl = ld.inc(s.dma_start(out=Ks[hb][:], in_=self.d_K[P * h:P * (h + 1), :]), 16)
            for ktl in range(T // P):
                wl = ld.inc(s.dma_start(
                    out=Vs[hb][:, P * ktl:P * (ktl + 1)],
                    in_=self.d_V[P * ktl:P * (ktl + 1), P * h:P * (h + 1)]), 16)
            wl = ld.inc(s.dma_start(out=Qs[hb][:], in_=self.d_Q[P * h:P * (h + 1), :]), 16)
            t.wait_ge(ld.h, wl)
            if h >= 1:
                t.wait_ge(snrm.h, h)
            base_exp = sexp.n
            base_pv = spv.n

            def emit_pv(kt, wv_):
                t.wait_ge(svm.h, wv_)
                smin_ = kt // 8
                q0_ = P * smin_
                if smin_ < 4:
                    segs = [(q0_, 512, ps_s0, 0, 39), (512, TOK, ps_s1, 512, 63)]
                else:
                    segs = [(q0_, TOK, ps_s1, 512, 63)]
                for (a, bnd, pss_, sb0, lastkt) in segs:
                    mms = t.matmul(pss_[:, a - sb0:bnd - sb0], self.ones_bf[:],
                                   Eb[kt % 2][:, a:bnd],
                                   start=(kt == 0), stop=(kt == lastkt))
                    if kt == 63:
                        ssb.inc(mms, 1)
                    mm = t.matmul(ps_O[:, a:bnd], Vs[hb][:, P * kt:P * (kt + 1)],
                                  Eb[kt % 2][:, a:bnd],
                                  start=(kt == 0), stop=(kt == lastkt))
                spv.inc(mm, 1)

            pend = None
            for kt in range(T // P):
                smin = kt // 8
                q0 = P * smin
                o = kt % 8
                Sb = ps_S[kt % 2]
                pieces = ([(q0, 512), (512, TOK)] if smin < 4 else [(q0, TOK)])
                if kt >= 2:
                    t.wait_ge(sexp.h, base_exp + kt - 1)
                for pi_, (a, bnd) in enumerate(pieces):
                    wm = smm.inc(t.matmul(Sb[:, a:bnd],
                                          Ks[hb][:, P * kt:P * (kt + 1)],
                                          Qs[hb][:, a:bnd],
                                          start=True, stop=True), 1)
                if pend is not None:
                    emit_pv(*pend)
                sc.wait_ge(smm.h, wm)
                if kt >= 2:
                    sc.wait_ge(spv.h, base_pv + kt - 1)
                we = sexp.inc(sc.activation(Eb[kt % 2][:, q0:TOK], Sb[:, q0:TOK],
                                            AF.Exp, scale=SCALE, bias=EXP_SHIFT), 1)
                v.wait_ge(sexp.h, we)
                wv_ = svm.inc(v.tensor_tensor(Eb[kt % 2][:, q0:q0 + P],
                                              Eb[kt % 2][:, q0:q0 + P],
                                              self.masks_s[:, P * o:P * (o + 1)],
                                              ALU.mult), 1)
                pend = (kt, wv_)
                if kt % 5 == 4:
                    next(self.dq_gen, None)
            emit_pv(*pend)
            v.wait_ge(spv.h, spv.n)
            v.wait_ge(ssb.h, ssb.n)
            v.wait_ge(smm.h, smm.n)
            v.tensor_copy(sums_sb[:, 0:512], ps_s0[:])
            v.tensor_copy(sums_sb[:, 512:TOK], ps_s1[:])
            wv_ = ssum.inc(v.reciprocal(recip[:], sums_sb[:]), 1)
            t.wait_ge(ssum.h, wv_)
            wbc0 = smm.inc(t.matmul(ps_m[0][:], self.ones_r32[:], recip[:, 0:512],
                                    start=True, stop=True), 1)
            wbc = smm.inc(t.matmul(ps_m[1][:], self.ones_r32[:], recip[:, 512:TOK],
                                   start=True, stop=True), 1)
            v.wait_ge(smm.h, wbc0)
            wv_ = ssum.inc(v.tensor_copy(bc_sb[:, 0:512], ps_m[0][:]), 1)
            v.wait_ge(smm.h, wbc)
            ssum.inc(v.tensor_copy(bc_sb[:, 512:TOK], ps_m[1][:]), 1)
            v.tensor_tensor(Oo[hb][:, 0:512], ps_O[:, 0:512], bc_sb[:, 0:512], ALU.mult)
            wn = snrm.inc(v.tensor_tensor(Oo[hb][:, 512:TOK], ps_O[:, 512:TOK],
                                          bc_sb[:, 512:TOK], ALU.mult), 1)
            s.wait_ge(snrm.h, wn)
            sso.inc(s.dma_start(out=self.d_O[P * h:P * (h + 1), :], in_=Oo[hb][:]), 16)
        self.ph5_done = sso

    def ph6_proj_ln2(self, scope):
        nc = self.nc
        s, g, v, sc, t = nc.sync, nc.gpsimd, nc.vector, nc.scalar, nc.tensor
        for eng in (s, g, v, sc, t):
            eng.wait_ge(self.ph5_done.h, self.ph5_done.n)
        x1 = self.sb("p6x1", [P, HC * TOK], F32, scope)
        with ExitStack() as pa:
            wp = self.sb("p6wp", [P, HC * C], BF16, pa)
            Os = self.sb("p6o", [P, HC * TOK], BF16, pa)
            xc = [self.sb(f"p6xc{i}", [P, 512], F32, pa) for i in range(2)]
            pps = [self.psum(f"p6ps{i}", [P, 512], F32, pa) for i in range(2)]
            ldw = self.sem("p6ldw")
            ldx = self.sem("p6ldx")
            mme = self.sem("p6mm")
            evs = self.sem("p6ev")
            sxo = self.sem("p6so")
            for hc in range(HC):
                ldw.inc(g.dma_start(out=wp[:, hc * C:(hc + 1) * C],
                                    in_=self.w_proj_p[P * hc:P * (hc + 1), :]), 16)
            for chl in range(HC):
                ldw.inc(s.dma_start(
                    out=Os[:, TOK * chl:TOK * (chl + 1)],
                    in_=self.d_O[P * chl:P * (chl + 1), :]), 16)
            t.wait_ge(ldw.h, ldw.n)
            grp = 0
            for oc in range(HC):
                for half in range(2):
                    pb = grp % 2
                    wlx = ldx.inc(s.dma_start(
                        out=xc[pb][:],
                        in_=self.d_xfm[P * oc:P * (oc + 1),
                                       512 * half:512 * (half + 1)]), 16)
                    if grp >= 2:
                        t.wait_ge(evs.h, grp - 1)
                    for hc in range(HC):
                        mm = t.matmul(pps[pb][:],
                                      wp[:, hc * C + P * oc: hc * C + P * (oc + 1)],
                                      Os[:, hc * TOK + 512 * half: hc * TOK + 512 * (half + 1)],
                                      start=(hc == 0), stop=(hc == HC - 1))
                    wm = mme.inc(mm, 1)
                    v.wait_ge(mme.h, wm)
                    v.wait_ge(ldx.h, wlx)
                    col = oc * TOK + 512 * half
                    we = evs.inc(v.tensor_tensor(x1[:, col:col + 512], pps[pb][:],
                                                 xc[pb][:], ALU.add), 1)
                    s.wait_ge(evs.h, we)
                    sxo.inc(s.dma_start(out=self.d_x1[P * oc:P * (oc + 1),
                                                      512 * half:512 * (half + 1)],
                                        in_=x1[:, col:col + 512]), 16)
                    grp += 1
            self.ph6_x1_done = sxo
        for eng in (s, g, v, sc, t):
            eng.wait_ge(self.ph6_x1_done.h, self.ph6_x1_done.n)
        with ExitStack() as pb_:
            sq6 = self.sb("p6sq", [P, 512], F32, pb_)
            strow_t = self.sb("p6st", [1, 8 * 512], F32, pb_)
            def strow(i):
                return strow_t[0:1, 512 * i:512 * (i + 1)]
            rstd_b = self.sb("p6rb", [P, 512], F32, pb_)
            nmr_b = self.sb("p6nb", [P, 512], F32, pb_)
            tmp6 = self.sb("p6tmp", [P, 512], F32, pb_)
            l2c = self.sb("p6l2", [P, 512], BF16, pb_)
            ps_m1 = self.psum("p6pm1", [1, 512], F32, pb_)
            ps_m2 = self.psum("p6pm2", [1, 512], F32, pb_)
            ps_bc = self.psum("p6pbc", [P, 512], F32, pb_)
            sact = self.sem("p6act")
            scm = self.sem("p6cm")
            spl = self.sem("p6pl")
            smb = self.sem("p6mb")
            sm1 = self.sem("p6m1")
            sso2 = self.sem("p6so2")
            for half in range(2):
                if half == 1:
                    t.wait_ge(scm.h, scm.n)
                for oc in range(HC):
                    col = oc * TOK + 512 * half
                    if half > 0 or oc > 0:
                        sc.wait_ge(smb.h, smb.n)
                    wa = sact.inc(sc.activation(sq6[:], x1[:, col:col + 512], AF.Square), 1)
                    t.wait_ge(sact.h, wa)
                    mm1 = t.matmul(ps_m1[:], self.ones_c32[:], x1[:, col:col + 512],
                                   start=(oc == 0), stop=(oc == HC - 1))
                    if oc == HC - 1:
                        sm1.inc(mm1, 1)
                    wm = smb.inc(t.matmul(ps_m2[:], self.ones_c32[:], sq6[:],
                                          start=(oc == 0), stop=(oc == HC - 1)), 1)
                v.wait_ge(smb.h, smb.n)
                v.wait_ge(sm1.h, sm1.n)
                v.tensor_copy(strow(0), ps_m1[:])
                wv_ = scm.inc(v.tensor_copy(strow(1), ps_m2[:]), 1)
                g.wait_ge(scm.h, wv_)
                g.tensor_scalar(strow(2), strow(0), 1.0 / C, None, ALU.mult)
                g.tensor_scalar(strow(3), strow(1), 1.0 / C, None, ALU.mult)
                g.tensor_tensor(strow(4), strow(2), strow(2), ALU.mult)
                g.tensor_tensor(strow(3), strow(3), strow(4), ALU.subtract)
                wp_ = spl.inc(g.tensor_scalar(strow(3), strow(3), EPS, None, ALU.add), 1)
                sc.wait_ge(spl.h, wp_)
                wa = sact.inc(sc.activation(strow(4), strow(3), AF.Ln), 1)
                sc.wait_ge(sact.h, wa)
                wa = sact.inc(sc.activation(strow(4), strow(4), AF.Exp, scale=-0.5), 1)
                g.wait_ge(sact.h, wa)
                g.tensor_tensor(strow(5), strow(2), strow(4), ALU.mult)
                wp_ = spl.inc(g.tensor_scalar(strow(5), strow(5), -1.0, None, ALU.mult), 1)
                t.wait_ge(spl.h, wp_)
                wm = smb.inc(t.matmul(ps_bc[:], self.ones_r32[:], strow(4),
                                      start=True, stop=True), 1)
                v.wait_ge(smb.h, wm)
                wv_ = scm.inc(v.tensor_copy(rstd_b[:], ps_bc[:]), 1)
                t.wait_ge(scm.h, wv_)
                wm = smb.inc(t.matmul(ps_bc[:], self.ones_r32[:], strow(5),
                                      start=True, stop=True), 1)
                v.wait_ge(smb.h, wm)
                scm.inc(v.tensor_copy(nmr_b[:], ps_bc[:]), 1)
                for oc in range(HC):
                    col = oc * TOK + 512 * half
                    if half > 0 or oc > 0:
                        v.wait_ge(sso2.h, sso2.n)
                    v.tensor_tensor(tmp6[:], x1[:, col:col + 512], rstd_b[:], ALU.mult)
                    wv_ = scm.inc(v.tensor_tensor(l2c[:], tmp6[:], nmr_b[:], ALU.add), 1)
                    s.wait_ge(scm.h, wv_)
                    sso2.inc(s.dma_start(out=self.d_ln2[P * oc:P * (oc + 1),
                                                        512 * half:512 * (half + 1)],
                                         in_=l2c[:]), 16)
            self.ph6_done = sso2

    def ph7_ffn(self, scope):
        nc = self.nc
        s, g, v, sc, t = nc.sync, nc.gpsimd, nc.vector, nc.scalar, nc.tensor
        engs = (s, g, v, sc, t)

        def barrier(sems):
            for eng in engs:
                for sem_ in sems:
                    eng.wait_ge(sem_.h, sem_.n)

        barrier([self.ph6_done, self.ph6_x1_done] + self.ph4_dones)
        yso = self.sem("p7yso")
        for half in range(2):
            if half == 1:
                barrier([yso])
            cols = 512 * half
            with ExitStack() as hs:
                inter = self.sb(f"p7int{half}", [P, NCC * 512], BF16, hs)
                sA = self.sem(f"p7sA{half}")
                with ExitStack() as pa:
                    ln2s = self.sb(f"p7l2{half}", [P, HC * 512], BF16, pa)
                    wgb = [self.sb(f"p7wg{half}_{i}", [P, HC * P], BF16, pa) for i in range(2)]
                    wub = [self.sb(f"p7wu{half}_{i}", [P, HC * P], BF16, pa) for i in range(2)]
                    gel = self.sb(f"p7gel{half}", [P, 512], BF16, pa)
                    ps_g = [self.psum(f"p7pg{half}_{i}", [P, 512], F32, pa) for i in range(2)]
                    ps_u = [self.psum(f"p7pu{half}_{i}", [P, 512], F32, pa) for i in range(2)]
                    ld = self.sem(f"p7ld{half}")
                    smm = self.sem(f"p7mm{half}")
                    sgel = self.sem(f"p7gl{half}")
                    for hcl in range(HC):
                        wl = ld.inc(s.dma_start(
                            out=ln2s[:, 512 * hcl:512 * (hcl + 1)],
                            in_=self.d_ln2[P * hcl:P * (hcl + 1),
                                           cols:cols + 512]), 16)
                    t.wait_ge(ld.h, wl)
                    for cc in range(NCC):
                        b = cc % 2
                        if cc >= 2:
                            g.wait_ge(smm.h, 2 * (cc - 1))
                        wl = ld.inc(g.dma_start(
                            out=wgb[b][:].rearrange("p (h f) -> p h f", f=P),
                            in_=self.d_Wg[cc].rearrange("h p f -> p h f")), 16)
                        wl = ld.inc(g.dma_start(
                            out=wub[b][:].rearrange("p (h f) -> p h f", f=P),
                            in_=self.d_Wu[cc].rearrange("h p f -> p h f")), 16)
                        t.wait_ge(ld.h, wl)
                        if cc >= 2:
                            t.wait_ge(sA.h, cc - 1)
                        for hc in range(HC):
                            gmm = t.matmul(ps_g[b][:], wgb[b][:, P * hc:P * (hc + 1)],
                                           ln2s[:, 512 * hc:512 * (hc + 1)],
                                           start=(hc == 0), stop=(hc == HC - 1))
                        wm = smm.inc(gmm, 1)
                        for hc in range(HC):
                            mm = t.matmul(ps_u[b][:], wub[b][:, P * hc:P * (hc + 1)],
                                          ln2s[:, 512 * hc:512 * (hc + 1)],
                                          start=(hc == 0), stop=(hc == HC - 1))
                        wm = smm.inc(mm, 1)
                        sc.wait_ge(smm.h, wm)
                        if cc >= 1:
                            sc.wait_ge(sA.h, cc)
                        wg_ = sgel.inc(sc.activation(gel[:], ps_g[b][:],
                                                     AF.Gelu_apprx_tanh), 1)
                        v.wait_ge(sgel.h, wg_)
                        sA.inc(v.tensor_tensor(inter[:, 512 * cc:512 * (cc + 1)],
                                               gel[:], ps_u[b][:], ALU.mult), 1)
                barrier([sA])
                if self.phase == 8 and half == 1:
                    dbs = self.sem("p7dbg1")
                    dbs.inc(s.dma_start(out=self.d_dbE[:, :],
                                        in_=inter[:, 56 * 512:56 * 512 + C]), 16)
                    dbs.inc(s.dma_start(out=self.d_dbi[:, :], in_=inter[:, :]), 16)
                    self.dbg1 = dbs
                out_s = self.sb(f"p7out{half}", [P, HC * 512], F32, hs)
                with ExitStack() as pb_:
                    x1c = [self.sb(f"p7x1{half}_{i}", [P, 512], F32, pb_) for i in range(2)]
                    wdb = [self.sb(f"p7wd{half}_{i}", [P, 8 * P], BF16, pb_) for i in range(2)]
                    ps_d = [self.psum(f"p7pd{half}_{i}", [P, 512], F32, pb_) for i in range(8)]
                    ldd = self.sem(f"p7ldd{half}")
                    smd = self.sem(f"p7md{half}")
                    sev = self.sem(f"p7ev{half}")
                    ldx = self.sem(f"p7lx{half}")
                    NIP = NCC + 7
                    for pass_ in range(2):
                        if pass_ == 1:
                            t.wait_ge(sev.h, 8)
                        for cc in range(NCC):
                            b = cc % 2
                            gi = pass_ * NCC + cc
                            if gi >= 2:
                                pp, cp = (gi - 2) // NCC, (gi - 2) % NCC
                                req = NIP * pp + (cp + 1 if cp < NCC - 1 else NIP)
                                g.wait_ge(smd.h, req)
                            for jl in range(8):
                                wl = ldd.inc(g.dma_start(
                                    out=wdb[b][:, P * jl:P * (jl + 1)],
                                    in_=self.d_Wd[cc, 8 * pass_ + jl]), 16)
                            t.wait_ge(ldd.h, wl)
                            for j in range(8):
                                mm = t.matmul(ps_d[j][:], wdb[b][:, P * j:P * (j + 1)],
                                              inter[:, 512 * cc:512 * (cc + 1)],
                                              start=(cc == 0), stop=(cc == NCC - 1))
                                if cc == NCC - 1:
                                    smd.inc(mm, 1)
                            if cc < NCC - 1:
                                smd.inc(mm, 1)
                        for j in range(8):
                            hh = 8 * pass_ + j
                            xb = j % 2
                            if 8 * pass_ + j >= 2:
                                s.wait_ge(sev.h, 8 * pass_ + j - 1)
                            wlx = ldx.inc(s.dma_start(
                                out=x1c[xb][:],
                                in_=self.d_x1[P * hh:P * (hh + 1), cols:cols + 512]), 16)
                            v.wait_ge(smd.h, NIP * (pass_ + 1))
                            v.wait_ge(ldx.h, wlx)
                            sev.inc(v.tensor_tensor(out_s[:, 512 * hh:512 * (hh + 1)],
                                                    ps_d[j][:], x1c[xb][:], ALU.add), 1)
                if self.phase == 8 and half == 1:
                    self.dbg1.inc(s.dma_start(out=self.d_dbg[P:2 * P, :], in_=out_s[:, 0:C]), 16)
                    self.dbg1.inc(s.dma_start(out=self.d_dbL[:, :],
                                              in_=inter[:, 56 * 512:56 * 512 + C]), 16)
                if self.phase >= 9:
                    if half == 0:
                        self.dbg9 = self.sem("p7dbg9")
                    for eng in (s,):
                        eng.wait_ge(sev.h, sev.n)
                    self.dbg9.inc(s.dma_start(out=self.d_dbo[P * half:P * (half + 1), :],
                                              in_=out_s[:, :]), 16)
                barrier([sev])
                if self.phase >= 9 and half == 1:
                    self.dbg9.inc(s.dma_start(out=self.d_dbi[:, :], in_=inter[:, :]), 16)
                with ExitStack() as pc_:
                    asm = self.sb(f"p7asm{half}", [P, C], F32, pc_)
                    ps_t = self.psum(f"p7pt{half}", [P, P], F32, pc_)
                    stp = self.sem(f"p7tp{half}")
                    scp = self.sem(f"p7cp{half}")
                    for t4 in range(4):
                        if t4 > 0:
                            t.wait_ge(yso.h, yso.n)
                        for hh in range(HC):
                            if hh > 0 or t4 > 0:
                                t.wait_ge(scp.h, scp.n)
                            wt = stp.inc(t.transpose(
                                ps_t[:], out_s[:, 512 * hh + P * t4: 512 * hh + P * (t4 + 1)],
                                self.ident3_s[:]), 1)
                            v.wait_ge(stp.h, wt)
                            if hh < 8:
                                dstc = asm[:, 256 * hh: 256 * (hh + 1): 2]
                            else:
                                dstc = asm[:, 256 * (hh - 8) + 1: 256 * (hh - 7): 2]
                            scp.inc(v.tensor_copy(dstc, ps_t[:]), 1)
                        s.wait_ge(scp.h, scp.n)
                        yso.inc(s.dma_start(
                            out=self.y[512 * half + P * t4: 512 * half + P * (t4 + 1), :],
                            in_=asm[:]), 16)
                        if self.phase == 8 and half == 1 and t4 == 0:
                            self.dbg1.inc(s.dma_start(out=self.d_dbg[3 * P:4 * P, :],
                                                      in_=asm[:]), 16)
        self.ph7_done = yso

    def epilogue(self, scope):
        nc = self.nc
        s, v = nc.sync, nc.vector
        fin = self.sem("fin")
        cv = self.sem("dbgcv")
        ld = self.sem("dbgld")

        def dump_dram_bf16(row, src_ap, w=C):
            dbge = self.sb(f"dbge{row}", [P, w], BF16, scope)
            dbgf = self.sb(f"dbgf{row}", [P, w], F32, scope)
            wl = ld.inc(s.dma_start(out=dbge[:], in_=src_ap), 16)
            v.wait_ge(ld.h, wl)
            wc = cv.inc(v.tensor_copy(dbgf[:], dbge[:]), 1)
            s.wait_ge(cv.h, wc)
            fin.inc(s.dma_start(out=self.y[row * P:(row + 1) * P, 0:w], in_=dbgf[:]), 16)

        def dump_dram_f32(row, src_ap, w=C):
            dbgf = self.sb(f"dbgg{row}", [P, w], F32, scope)
            wl = ld.inc(s.dma_start(out=dbgf[:], in_=src_ap), 16)
            s.wait_ge(ld.h, wl)
            fin.inc(s.dma_start(out=self.y[row * P:(row + 1) * P, 0:w], in_=dbgf[:]), 16)

        if self.phase >= 7:
            s.wait_ge(self.ph7_done.h, self.ph7_done.n)
            if self.phase == 8:
                s.wait_ge(self.dbg1.h, self.dbg1.n)
                dump_dram_bf16(0, self.d_dbE[:, :])
                dump_dram_f32(1, self.d_dbg[P:2 * P, :])
                dump_dram_bf16(2, self.d_dbL[:, :])
                dump_dram_f32(3, self.d_dbg[3 * P:4 * P, :])
        elif self.phase >= 6:
            s.wait_ge(self.ph6_done.h, self.ph6_done.n)
            s.wait_ge(self.ph6_x1_done.h, self.ph6_x1_done.n)
            dump_dram_f32(0, self.d_x1[0:P, :], TOK)
            dump_dram_bf16(1, self.d_ln2[0:P, :], TOK)
            dump_dram_f32(2, self.d_x1[5 * P:6 * P, :], TOK)
        elif self.phase >= 5:
            s.wait_ge(self.ph5_done.h, self.ph5_done.n)
            dump_dram_bf16(0, self.d_O[0:P, :], TOK)
            dump_dram_bf16(1, self.d_O[P:2 * P, :], TOK)
        elif self.phase >= 4:
            for sem_ in self.ph4_dones:
                s.wait_ge(sem_.h, sem_.n)

            def dump_blocks(row, dest, blk):
                dbge = self.sb(f"dbge{row}", [P, C], BF16, scope)
                dbgf = self.sb(f"dbgf{row}", [P, C], F32, scope)
                wl = ld.n
                for cc in range(16):
                    wl = ld.inc(s.dma_start(out=dbge[:, P * cc:P * (cc + 1)],
                                            in_=dest[cc, blk]), 16)
                v.wait_ge(ld.h, wl)
                wc = cv.inc(v.tensor_copy(dbgf[:], dbge[:]), 1)
                s.wait_ge(cv.h, wc)
                fin.inc(s.dma_start(out=self.y[row * P:(row + 1) * P, :], in_=dbgf[:]), 16)

            dump_blocks(0, self.d_Wg, 3)
            dump_blocks(1, self.d_Wu, 11)
            dump_blocks(2, self.d_Wd, 5)
        elif self.phase >= 3:
            for sem_ in (self.ph3_done, self.ph3_k_done, self.ph3_q_done):
                s.wait_ge(sem_.h, sem_.n)
            dump_dram_bf16(0, self.d_K[0:P, 0:C])
            dump_dram_bf16(1, self.d_V[0:P, 0:C])
            dump_dram_bf16(2, self.d_Q[0:P, :], TOK)
        elif self.phase >= 2:
            s.wait_ge(self.ph2_done.h, self.ph2_done.n)
            dump_dram_bf16(0, self.d_ln1fmT[0:P, 0:C])
        else:
            s.wait_ge(self.ph1_full_done.h, self.ph1_full_done.n)
            s.wait_ge(self.ph1_own_done.h, self.ph1_own_done.n)
            dump_dram_bf16(0, self.d_ln1tm[0:P, :])
            dump_dram_bf16(1, self.d_lo[0:P, :], TOK)
            dump_dram_f32(2, self.d_xfm[0:P, :], TOK)
        s.wait_ge(fin.h, fin.n)


def kernel(**inputs):
    b = Builder(phase=7)
    nc = b.build()
    in_maps = [host_prep(inputs, core) for core in range(8)]
    from concourse.bass_utils import run_bass_kernel_spmd
    res = run_bass_kernel_spmd(nc, in_maps, core_ids=list(range(8)))
    out = np.empty((T, C), np.float32)
    for core in range(8):
        out[own_rows(core)] = np.asarray(res.results[core]["y"], np.float32)
    return out.reshape(1, T, C)
